# version 12
# baseline (speedup 1.0000x reference)
import contextlib
import numpy as np
import ml_dtypes
import concourse.bass as bass
import concourse.mybir as mybir
from concourse.bass_utils import run_bass_kernel_spmd

F32 = mybir.dt.float32
BF16 = mybir.dt.bfloat16
U32 = mybir.dt.uint32
AF = mybir.ActivationFunctionType
ALU = mybir.AluOpType
AX = mybir.AxisListType

ENGINES = ("sync", "scalar", "vector", "tensor", "gpsimd")
MAGIC = float(1.5 * 2 ** 23)
NCORES = 8
TCORE = 2048
HALO = 256
D = 1024
BF = ml_dtypes.bfloat16


class Emitter:

    def __init__(self, nc, serialize=False):
        self.nc = nc
        self.serialize = serialize
        self.steps = []
        self.eng_steps = {e: [] for e in ENGINES}
        self.last_writer = {}
        self.readers = {}
        self.eng_count = {e: 0 for e in ENGINES}
        self.grp_count = {}

    @staticmethod
    def _key(x):
        t = getattr(x, "tensor", None)
        if t is not None:
            return t.name
        n = getattr(x, "name", None)
        return n if n is not None else x

    def emit(self, engine, fn, reads=(), writes=(), name="", dma=False,
             group=None, extra_deps=(), no_selfdep=False):
        idx = len(self.steps)
        deps = set(extra_deps)
        if self.serialize and idx > 0:
            deps.add(idx - 1)
        rk = [self._key(x) for x in reads]
        wk = [self._key(x) for x in writes]
        for k in rk:
            if k in self.last_writer:
                deps.add(self.last_writer[k])
        for k in wk:
            if k in self.last_writer:
                deps.add(self.last_writer[k])
            for r in self.readers.get(k, ()):
                deps.add(r)
        for k in rk:
            self.readers.setdefault(k, []).append(idx)
        for k in wk:
            self.last_writer[k] = idx
            self.readers[k] = []
        if dma:
            grp = group or (engine + "_dma")
            self.grp_count[grp] = self.grp_count.get(grp, 0) + 16
            sem_val = None
        else:
            grp = None
            self.eng_count[engine] += 1
            sem_val = self.eng_count[engine]
        st = dict(engine=engine, fn=fn, deps=sorted(deps), name=name, dma=dma,
                  grp=grp, sem_val=sem_val, idx=idx, no_selfdep=no_selfdep)
        self.steps.append(st)
        self.eng_steps[engine].append(st)
        return idx

    def _resolve_waits(self):
        run = {}
        for st in self.steps:
            if st["dma"]:
                run[st["grp"]] = run.get(st["grp"], 0) + 16
                st["grp_val"] = run[st["grp"]]
        for st in self.steps:
            waits = {}
            if st["dma"] and st["grp_val"] > 16:
                waits[("g", st["grp"])] = st["grp_val"] - 16
            for d in st["deps"]:
                ds = self.steps[d]
                if ds["dma"]:
                    key = ("g", ds["grp"])
                    v = ds["grp_val"]
                else:
                    if ds["engine"] == st["engine"] and st["no_selfdep"]:
                        continue
                    key = ("e", ds["engine"])
                    v = ds["sem_val"]
                waits[key] = max(waits.get(key, 0), v)
            st["waits"] = waits

    def build(self, block, ctx):
        nc = self.nc
        self._resolve_waits()
        sems = {}
        for e in ENGINES:
            if any(not s["dma"] for s in self.eng_steps[e]):
                sems[("e", e)] = ctx.enter_context(nc.semaphore(f"sem_{e}"))
        for g in self.grp_count:
            sems[("g", g)] = ctx.enter_context(nc.semaphore(f"sem_{g}"))

        def make_body(e):
            def body(eng):
                for st in self.eng_steps[e]:
                    for key, v in sorted(st["waits"].items()):
                        eng.wait_ge(sems[key], v)
                    inst = st["fn"](eng)
                    if inst is None:
                        raise RuntimeError(f"step {st['name']}: fn returned None")
                    if st["dma"]:
                        inst.then_inc(sems[("g", st["grp"])], 16)
                    else:
                        inst.then_inc(sems[("e", e)], 1)
            return body

        for e in ENGINES:
            if self.eng_steps[e]:
                getattr(block, e)(make_body(e))
        return sems


class Prog:
    def __init__(self):
        self.nc = bass.Bass("TRN2", target_bir_lowering=False)
        self.em = Emitter(self.nc)
        self.ctx = contextlib.ExitStack()
        self.outs = []

    def inp(self, name, shape, dt=F32):
        return self.nc.dram_tensor(name, shape, dt, kind="ExternalInput").ap()

    def out(self, name, shape, dt=F32):
        t = self.nc.dram_tensor(name, shape, dt, kind="ExternalOutput").ap()
        self.outs.append(name)
        return t

    def dram(self, name, shape, dt):
        return self.nc.dram_tensor(name, shape, dt).ap()

    def sb(self, name, shape, dt):
        return self.ctx.enter_context(self.nc.sbuf_tensor(name, shape, dt))

    def ps(self, name, shape, dt=F32):
        return self.ctx.enter_context(self.nc.psum_tensor(name, shape, dt))

    def begin(self):
        self.block = self.ctx.enter_context(self.nc.Block())

    def finish(self):
        self.em.emit("sync", lambda e: e.nop(), name="final_barrier",
                     extra_deps=[s["idx"] for s in self.em.steps if s["dma"]])
        self.em.build(self.block, self.ctx)
        self.ctx.close()
        return self.nc

    def dma(self, dst, src, group="d", name="", eng="sync"):
        self.em.emit(eng, lambda e: e.dma_start(out=dst, in_=src),
                     reads=[src], writes=[dst], dma=True, group=group, name=name)

    def tt(self, out, a, b, op, name="", eng="vector"):
        self.em.emit(eng, lambda e: e.tensor_tensor(out=out, in0=a, in1=b, op=op),
                     reads=[a, b], writes=[out], name=name)

    def ts(self, out, a, s1, s2=None, op0=ALU.mult, op1=ALU.bypass, name="",
           eng="vector"):
        ins = [a] + [s for s in (s1, s2) if hasattr(s, "tensor")]
        if s2 is None:
            self.em.emit(eng, lambda e: e.tensor_scalar(
                out=out, in0=a, scalar1=s1, scalar2=None, op0=op0),
                reads=ins, writes=[out], name=name)
        else:
            self.em.emit(eng, lambda e: e.tensor_scalar(
                out=out, in0=a, scalar1=s1, scalar2=s2, op0=op0, op1=op1),
                reads=ins, writes=[out], name=name)

    def stt(self, out, in0, scalar, in1, op0, op1, name="", eng="vector"):
        ins = [in0, in1] + ([scalar] if hasattr(scalar, "tensor") else [])
        self.em.emit(eng, lambda e: e.scalar_tensor_tensor(
            out=out, in0=in0, scalar=scalar, in1=in1, op0=op0, op1=op1),
            reads=ins, writes=[out], name=name)

    def act(self, out, in_, func, bias=0.0, scale=1.0, accum_out=None, name=""):
        ins = [in_] + [s for s in (bias, scale) if hasattr(s, "tensor")]
        ws = [out] + ([accum_out] if accum_out is not None else [])
        self.em.emit("scalar", lambda e: e.activation(
            out=out, in_=in_, func=func, bias=bias, scale=scale,
            accum_out=accum_out), reads=ins, writes=ws, name=name)

    def red(self, out, in_, op=ALU.add, absval=None, name="", negate=None):
        self.em.emit("vector", lambda e: e.tensor_reduce(
            out=out, in_=in_, axis=AX.X, op=op, apply_absolute_value=absval,
            negate=negate), reads=[in_], writes=[out], name=name)

    def rcp(self, out, in_, name=""):
        self.em.emit("vector", lambda e: e.reciprocal(out=out, in_=in_),
                     reads=[in_], writes=[out], name=name)

    def cpy(self, out, in_, name="", eng="vector"):
        if eng == "scalar":
            self.em.emit(eng, lambda e: e.copy(out=out, in_=in_),
                         reads=[in_], writes=[out], name=name)
        else:
            self.em.emit(eng, lambda e: e.tensor_copy(out=out, in_=in_),
                         reads=[in_], writes=[out], name=name)

    def mset(self, t, v, name=""):
        self.em.emit("vector", lambda e: e.memset(t, v), writes=[t], name=name)

    def mm(self, psum, pairs, name="", extra_reads=(), start=True, stop=True):
        def fn(e):
            last = None
            n = len(pairs)
            for k, (l, r) in enumerate(pairs):
                last = e.matmul(out=psum, lhsT=l, rhs=r,
                                start=(start and k == 0),
                                stop=(stop and k == n - 1))
            return last
        rd = [x for pr in pairs for x in pr] + list(extra_reads)
        self.em.emit("tensor", fn, reads=rd, writes=[psum], name=name)

    def tr(self, psum, in_, ident, name=""):
        self.em.emit("tensor", lambda e: e.transpose(
            out=psum, in_=in_, identity=ident),
            reads=[in_, ident], writes=[psum], name=name)

    def scan(self, out, d0, d1, init, name="", op0=ALU.mult, op1=ALU.add):
        ins = [d0, d1] + ([init] if hasattr(init, "tensor") else [])
        self.em.emit("vector", lambda e: e.tensor_tensor_scan(
            out=out, data0=d0, data1=d1, initial=init,
            op0=op0, op1=op1), reads=ins, writes=[out], name=name)


def emit_ln(p, x_ap, scr, out_ap, width, f32_out_ap=None):
    mean, xc, sq, var = scr["mean"], scr["xc"], scr["sq"], scr["var"]
    p.red(mean[:], x_ap, name="ln_sum")
    p.ts(mean[:], mean[:], 1.0 / width)
    p.ts(xc[:, 0:width], x_ap, mean[:], op0=ALU.subtract)
    p.act(sq[:, 0:width], xc[:, 0:width], AF.Square, accum_out=var[:])
    p.ts(var[:], var[:], 1.0 / width, 1e-5, op0=ALU.mult, op1=ALU.add)
    p.act(var[:], var[:], AF.Sqrt)
    p.rcp(var[:], var[:])
    p.ts(out_ap, xc[:, 0:width], var[:], op0=ALU.mult)
    if f32_out_ap is not None:
        p.ts(f32_out_ap, xc[:, 0:width], var[:], op0=ALU.mult)


def emit_actquant(p, xn_ap, scr, out_bf16_ap, sc_col, width, int_view,
                  back_s, frac_s):
    qs = scr["qs"]
    p.red(sc_col[:], xn_ap, absval=True, op=ALU.max, name="aq_amax")
    p.ts(sc_col[:], sc_col[:], 1e-5, op0=ALU.max)
    p.rcp(qs[:], sc_col[:])
    p.ts(qs[:], qs[:], 127.0)
    p.ts(xn_ap, xn_ap, qs[:], op0=ALU.mult)
    emit_round(p, xn_ap, out_bf16_ap, width, int_view, back_s, frac_s,
               127.0, -128.0)
    p.ts(sc_col[:], sc_col[:], 1.0 / 127.0)


def emit_round(p, xs_ap, out_ap, width, int_view, back_s, frac_s, cmax, cmin):
    iw = int_view.shape[1]
    for c0 in range(0, width, iw):
        c1 = min(c0 + iw, width)
        w = c1 - c0
        xs = xs_ap[:, c0:c1]
        iv = int_view[:, 0:w]
        bk = back_s[:, 0:w]
        fr = frac_s[:, 0:w]
        oc = out_ap[:, c0:c1]
        p.cpy(iv, xs)
        p.cpy(bk, iv)
        p.tt(fr, xs, bk, ALU.subtract)
        p.stt(bk, fr, 0.5, bk, op0=ALU.is_gt, op1=ALU.add)
        p.ts(oc, fr, -0.5, op0=ALU.is_lt)
        p.tt(oc, bk, oc, ALU.subtract)
        p.ts(oc, oc, cmax, cmin, op0=ALU.min, op1=ALU.max)


def emit_colsum128(p, col_ap, scr32, out11, name=""):
    ones = scr32["ones128"]
    ps11 = scr32["ps11"]
    p.mm(ps11[:], [(col_ap, ones[:, 0:1])], name=name or "colsum")
    p.cpy(out11, ps11[:])


def build_ffn():
    INNER = 4096
    NT = TCORE // 128
    p = Prog()
    resid = p.inp("resid", [TCORE, D])
    w_gate = p.inp("w_gate", [INNER, D])
    w_up = p.inp("w_up", [INNER, D])
    w_down = p.inp("w_down", [D, INNER])
    identb = p.inp("identb", [128, 128], BF16)
    resid2 = p.out("resid2", [TCORE, D])
    o_tgT = p.out("o_tgT", [D, INNER], BF16)
    o_tuT = p.out("o_tuT", [D, INNER], BF16)
    o_tdT = p.out("o_tdT", [INNER, D], BF16)
    o_scal = p.out("o_scal", [1, 4])
    d_gact = p.dram("d_gact", [TCORE, INNER], BF16)
    d_guqT = p.dram("d_guqT", [INNER, TCORE], BF16)

    xld = [p.sb(f"xld{i}", [128, D], F32) for i in range(2)]
    xq = p.sb("xq", [128, D], BF16)
    xqT = [p.sb(f"xqT{j}", [128, TCORE], BF16) for j in range(D // 128)]
    gu = p.sb("gu", [128, INNER], F32)
    i32s = p.sb("i32s", [128, 1024], mybir.dt.int32)
    scr = dict(mean=p.sb("mean", [128, 1], F32), xc=p.sb("xc", [128, D], F32),
               sq=gu, var=p.sb("var", [128, 1], F32),
               qs=p.sb("qs", [128, 1], F32))
    scr32 = dict(ones128=p.sb("ones128", [128, 1], F32),
                 ps11=p.ps("ps11", [1, 1], F32))
    sc_x = [p.sb(f"scx{i}", [128, 1], F32) for i in range(NT)]
    sc_gu = [p.sb(f"scgu{i}", [128, 1], F32) for i in range(NT)]
    ident = p.sb("ident", [128, 128], BF16)
    psT = p.ps("psT", [128, 128], BF16)
    psT2 = p.ps("psT2", [128, 128], BF16)
    psTr = [psT, psT2]
    wld = [p.sb("wld0", [128, INNER], F32)]
    wtb = p.sb("wtb", [128, INNER], BF16)
    wabs = p.sb("wabs", [128, 32], F32)
    wred = p.sb("wred", [128, 1], F32)
    wmean = p.sb("wmean", [1, 1], F32)
    wrecip = p.sb("wrecip", [1, 1], F32)
    wcol = p.sb("wcol", [128, 1], F32)
    mcol = p.sb("mcol", [128, 1], F32)
    scc = p.sb("scc", [128, 1], F32)
    onesf = p.sb("onesf", [1, 128], F32)
    psB = p.ps("psB", [128, 1], F32)
    wTb = [[p.sb(f"wTb{v}_{j}", [128, 512], BF16) for j in range(D // 128)]
           for v in range(2)]
    wTd = [p.sb(f"wTd{j}", [128, D], BF16) for j in range(INNER // 128)]
    pm = [p.ps(f"pm{b}", [128, 512], F32) for b in range(4)]
    gact = p.sb("gact", [128, INNER], BF16)
    guq = p.sb("guq", [128, INNER], BF16)
    guqs4 = [p.sb(f"guqs{v}", [128, 128], BF16) for v in range(4)]
    rtile = p.sb("rtile", [128, D], F32)
    ot = p.sb("ot", [128, D], F32)

    p.begin()
    p.dma(ident[:], identb[:], group="cst")
    p.mset(onesf[:], 1.0)
    p.mset(scr32["ones128"][:], 1.0)

    for i in range(NT):
        x = xld[i % 2]
        p.dma(x[:], resid[i * 128:(i + 1) * 128, :], group=f"ldx{i % 2}")
        emit_ln(p, x[:], scr, scr["xc"][:, 0:D], D)
        emit_actquant(p, scr["xc"][:, 0:D], scr, xq[:], sc_x[i], D,
                       i32s[:], rtile[:], ot[:])
        for j in range(D // 128):
            p.tr(psT[:], xq[:, j * 128:(j + 1) * 128], ident[:], name=f"trx{i}_{j}")
            p.cpy(xqT[j][:, i * 128:(i + 1) * 128], psT[:])

    def w_meanpass(w_dram, nrows, ncols, denom):
        ntile = nrows // 128
        for i in range(ntile):
            wl = wld[0]
            p.dma(wl[:, 0:ncols], w_dram[i * 128:(i + 1) * 128, :],
                  group="ldw0")
            p.red(wabs[:, i:i + 1], wl[:, 0:ncols], absval=True, op=ALU.add)
        p.red(wred[:], wabs[:, 0:ntile], op=ALU.add)
        emit_colsum128(p, wred[:], scr32, wmean[:, 0:1])
        p.ts(wmean[:, 0:1], wmean[:, 0:1], 1.0 / denom, 1e-5,
             op0=ALU.mult, op1=ALU.max)
        p.rcp(wrecip[:, 0:1], wmean[:, 0:1])
        p.mm(psB[:], [(onesf[:], wrecip[:, 0:1])], name="bc_wr")
        p.cpy(wcol[:], psB[:])
        p.mm(psB[:], [(onesf[:], wmean[:, 0:1])], name="bc_wm")
        p.cpy(mcol[:], psB[:])

    def w_tern(wl_ap, ncols):
        p.ts(gu[:, 0:ncols], wl_ap, wcol[:], op0=ALU.mult)
        emit_round(p, gu[:, 0:ncols], wtb[:, 0:ncols], ncols, i32s[:],
                   rtile[:], ot[:], 1.0, -1.0)

    d_uact = p.dram("d_uact", [TCORE, INNER], BF16)
    for wi, (which, wsrc, dst) in enumerate((("g", w_gate, d_gact),
                                             ("u", w_up, d_uact))):
        w_meanpass(wsrc, INNER, D, INNER * D)
        p.dma(o_scal[:, wi:wi + 1], wmean[:, 0:1], group="sscal")
        oT = o_tgT if which == "g" else o_tuT
        for b in range(8):
            wt = wTb[b % 2]
            for r in range(4):
                row0 = b * 512 + r * 128
                wl = wld[0]
                p.dma(wl[:, 0:D], wsrc[row0:row0 + 128, :], group="ldw0")
                w_tern(wl[:, 0:D], D)
                for j in range(D // 128):
                    pT = psTr[j % 2]
                    p.tr(pT[:], wtb[:, j * 128:(j + 1) * 128], ident[:])
                    p.cpy(wt[j][:, r * 128:(r + 1) * 128], pT[:])
            for j in range(D // 128):
                p.dma(oT[j * 128:(j + 1) * 128, b * 512:(b + 1) * 512],
                      wt[j][:], group="swt", eng="gpsimd")
            for i in range(NT):
                ps = pm[i % 4]
                p.mm(ps[:], [(xqT[k][:, i * 128:(i + 1) * 128], wt[k][:])
                             for k in range(8)], name=f"mm{which}{b}_{i}")
                p.tt(scc[:], sc_x[i][:], mcol[:], ALU.mult)
                fn = AF.Silu if which == "g" else AF.Copy
                p.act(gact[:, 0:512], ps[:], fn, scale=scc[:])
                p.dma(dst[i * 128:(i + 1) * 128, b * 512:(b + 1) * 512],
                      gact[:, 0:512], group="sga", eng="gpsimd")

    uact = p.sb("uact", [128, INNER], BF16)
    for i in range(NT):
        p.dma(gact[:, 0:INNER], d_gact[i * 128:(i + 1) * 128, :], group="lga",
              eng="gpsimd")
        p.dma(uact[:, 0:INNER], d_uact[i * 128:(i + 1) * 128, :], group="lua",
              eng="gpsimd")
        p.tt(gu[:, 0:INNER], gact[:, 0:INNER], uact[:, 0:INNER], ALU.mult)
        emit_actquant(p, gu[:, 0:INNER], scr, guq[:], sc_gu[i], INNER,
                      i32s[:], rtile[:], ot[:])
        for j in range(INNER // 128):
            pT = psTr[j % 2]
            gq = guqs4[j % 4]
            p.tr(pT[:], guq[:, j * 128:(j + 1) * 128], ident[:])
            p.cpy(gq[:], pT[:])
            p.dma(d_guqT[j * 128:(j + 1) * 128, i * 128:(i + 1) * 128],
                  gq[:], group=f"sgq{j % 2}", eng="gpsimd")

    w_meanpass(w_down, D, INNER, D * INNER)
    p.dma(o_scal[:, 2:3], wmean[:, 0:1], group="sscal")
    p.mset(scr["qs"][0:1, 0:1], 0.0)
    p.dma(o_scal[:, 3:4], scr["qs"][0:1, 0:1], group="sscal")
    for r in range(8):
        wl = wld[0]
        p.dma(wl[:, 0:INNER], w_down[r * 128:(r + 1) * 128, :], group="ldw0")
        w_tern(wl[:, 0:INNER], INNER)
        for j in range(INNER // 128):
            pT = psTr[j % 2]
            p.tr(pT[:], wtb[:, j * 128:(j + 1) * 128], ident[:])
            p.cpy(wTd[j][:, r * 128:(r + 1) * 128], pT[:])
    for j in range(INNER // 128):
        p.dma(o_tdT[j * 128:(j + 1) * 128, :], wTd[j][:], group="swt",
              eng="gpsimd")
    guqTl = [xqT[v // 4][:, (v % 4) * 128:(v % 4) * 128 + 128] for v in range(32)]
    for i in range(NT):
        p.dma(rtile[:], resid[i * 128:(i + 1) * 128, :], group="ldr")
        for k in range(INNER // 128):
            p.dma(guqTl[k], d_guqT[k * 128:(k + 1) * 128, i * 128:(i + 1) * 128],
                  group=f"lgq{k % 4}")
        for b in range(2):
            ps = pm[b % 4]
            p.mm(ps[:], [(guqTl[k], wTd[k][:, b * 512:(b + 1) * 512])
                         for k in range(32)], name=f"mmd{i}_{b}")
            p.tt(scc[:], sc_gu[i][:], mcol[:], ALU.mult)
            p.ts(scc[:], scc[:], 0.5)
            p.stt(ot[:, b * 512:(b + 1) * 512], ps[:], scc[:],
                  rtile[:, b * 512:(b + 1) * 512], op0=ALU.mult, op1=ALU.add)
        p.dma(resid2[i * 128:(i + 1) * 128, :], ot[:, 0:D], group="st")
    nc = p.finish()
    return nc, p.outs






def build_ffn2():
    INNER = 4096
    NT = TCORE // 128
    p = Prog()
    resid = p.inp("resid", [TCORE, D])
    tgT = p.inp("tgT", [D, INNER], BF16)
    tuT = p.inp("tuT", [D, INNER], BF16)
    tdT = p.inp("tdT", [INNER, D], BF16)
    wscal = p.inp("wscal", [1, 4])
    identb = p.inp("identb", [128, 128], BF16)
    resid2 = p.out("resid2", [TCORE, D])
    d_gact = p.dram("d_gact", [TCORE, INNER], BF16)
    d_uact = p.dram("d_uact", [TCORE, INNER], BF16)
    d_guqT = p.dram("d_guqT", [INNER, TCORE], BF16)

    xld = [p.sb(f"xld{i}", [128, D], F32) for i in range(2)]
    xq = p.sb("xq", [128, D], BF16)
    xqT = [p.sb(f"xqT{j}", [128, TCORE], BF16) for j in range(D // 128)]
    gu = p.sb("gu", [128, INNER], F32)
    i32s = p.sb("i32s", [128, 1024], mybir.dt.int32)
    scr = dict(mean=p.sb("mean", [128, 1], F32), xc=p.sb("xc", [128, D], F32),
               sq=gu, var=p.sb("var", [128, 1], F32),
               qs=p.sb("qs", [128, 1], F32))
    sc_x = [p.sb(f"scx{i}", [128, 1], F32) for i in range(NT)]
    sc_gu = [p.sb(f"scgu{i}", [128, 1], F32) for i in range(NT)]
    ident = p.sb("ident", [128, 128], BF16)
    psT = p.ps("psT", [128, 128], BF16)
    psT2 = p.ps("psT2", [128, 128], BF16)
    psTr = [psT, psT2]
    wsl = p.sb("wsl", [1, 4], F32)
    mcol = [p.sb(f"mcol{v}", [128, 1], F32) for v in range(3)]
    scc = p.sb("scc", [128, 1], F32)
    onesf = p.sb("onesf", [1, 128], F32)
    psB = p.ps("psB", [128, 1], F32)
    wTb = [[p.sb(f"wTb{v}_{j}", [128, 512], BF16) for j in range(D // 128)]
           for v in range(2)]
    wTd = [p.sb(f"wTd{j}", [128, D], BF16) for j in range(INNER // 128)]
    pm = [p.ps(f"pm{b}", [128, 512], F32) for b in range(4)]
    gact = p.sb("gact", [128, INNER], BF16)
    uact = p.sb("uact", [128, INNER], BF16)
    guq = p.sb("guq", [128, INNER], BF16)
    guqs4 = [p.sb(f"guqs{v}", [128, 128], BF16) for v in range(4)]
    rtile = p.sb("rtile", [128, D], F32)
    ot = p.sb("ot", [128, D], F32)
    guqTl = [xqT[v // 4][:, (v % 4) * 128:(v % 4) * 128 + 128] for v in range(32)]

    p.begin()
    p.dma(ident[:], identb[:], group="cst")
    p.mset(onesf[:], 1.0)
    p.dma(wsl[:], wscal[:, :], group="cst")
    for v in range(3):
        p.mm(psB[:], [(onesf[:], wsl[:, v:v + 1])], name=f"bcm{v}")
        p.cpy(mcol[v][:], psB[:])

    for i in range(NT):
        x = xld[i % 2]
        p.dma(x[:], resid[i * 128:(i + 1) * 128, :], group=f"ldx{i % 2}")
        emit_ln(p, x[:], scr, scr["xc"][:, 0:D], D)
        emit_actquant(p, scr["xc"][:, 0:D], scr, xq[:], sc_x[i], D,
                      i32s[:], rtile[:], ot[:])
        for j in range(D // 128):
            pT = psTr[j % 2]
            p.tr(pT[:], xq[:, j * 128:(j + 1) * 128], ident[:])
            p.cpy(xqT[j][:, i * 128:(i + 1) * 128], pT[:])

    for wi, (which, wT_dram, dst) in enumerate((("g", tgT, d_gact),
                                                ("u", tuT, d_uact))):
        for b in range(8):
            wt = wTb[b % 2]
            for j in range(D // 128):
                p.dma(wt[j][:], wT_dram[j * 128:(j + 1) * 128,
                                        b * 512:(b + 1) * 512],
                      group=f"lw{j % 2}")
            for i in range(NT):
                ps = pm[i % 4]
                p.mm(ps[:], [(xqT[k][:, i * 128:(i + 1) * 128], wt[k][:])
                             for k in range(8)], name=f"mm{which}{b}_{i}")
                p.tt(scc[:], sc_x[i][:], mcol[wi][:], ALU.mult)
                fn = AF.Silu if which == "g" else AF.Copy
                p.act(gact[:, 0:512], ps[:], fn, scale=scc[:])
                p.dma(dst[i * 128:(i + 1) * 128, b * 512:(b + 1) * 512],
                      gact[:, 0:512], group="sga", eng="gpsimd")

    for i in range(NT):
        p.dma(gact[:, 0:INNER], d_gact[i * 128:(i + 1) * 128, :], group="lga",
              eng="gpsimd")
        p.dma(uact[:, 0:INNER], d_uact[i * 128:(i + 1) * 128, :], group="lua",
              eng="gpsimd")
        p.tt(gu[:, 0:INNER], gact[:, 0:INNER], uact[:, 0:INNER], ALU.mult)
        emit_actquant(p, gu[:, 0:INNER], scr, guq[:], sc_gu[i], INNER,
                      i32s[:], rtile[:], ot[:])
        for j in range(INNER // 128):
            pT = psTr[j % 2]
            gq = guqs4[j % 4]
            p.tr(pT[:], guq[:, j * 128:(j + 1) * 128], ident[:])
            p.cpy(gq[:], pT[:])
            p.dma(d_guqT[j * 128:(j + 1) * 128, i * 128:(i + 1) * 128],
                  gq[:], group=f"sgq{j % 2}", eng="gpsimd")

    for j in range(INNER // 128):
        p.dma(wTd[j][:], tdT[j * 128:(j + 1) * 128, :], group=f"lwd{j % 2}")
    for i in range(NT):
        tsl = slice(i * 128, (i + 1) * 128)
        p.dma(rtile[:], resid[tsl, :], group="ldr")
        for k in range(INNER // 128):
            p.dma(guqTl[k], d_guqT[k * 128:(k + 1) * 128, i * 128:(i + 1) * 128],
                  group=f"lgq{k % 4}")
        for b in range(2):
            ps = pm[b % 4]
            p.mm(ps[:], [(guqTl[k], wTd[k][:, b * 512:(b + 1) * 512])
                         for k in range(32)], name=f"mmd{i}_{b}")
            p.tt(scc[:], sc_gu[i][:], mcol[2][:], ALU.mult)
            p.ts(scc[:], scc[:], 0.5)
            p.stt(ot[:, b * 512:(b + 1) * 512], ps[:], scc[:],
                  rtile[:, b * 512:(b + 1) * 512], op0=ALU.mult, op1=ALU.add)
        p.dma(resid2[tsl, :], ot[:, 0:D], group="st")
    nc = p.finish()
    return nc, p.outs


def build_moe():
    E_ = 8
    NT = TCORE // 128
    p = Prog()
    resid = p.inp("resid", [TCORE, D])
    router_w = p.inp("router_w", [D, E_])
    router_b = p.inp("router_b", [1, E_])
    expert_w = p.inp("expert_w", [E_, D, D])
    expert_b = p.inp("expert_b", [E_, D])
    identb = p.inp("identb", [128, 128], BF16)
    resid2 = p.out("resid2", [TCORE, D])

    xld = [p.sb(f"xld{i}", [128, D], F32) for i in range(2)]
    xn = p.sb("xn", [128, D], BF16)
    xnT = [p.sb(f"xnT{j}", [128, TCORE], BF16) for j in range(D // 128)]
    scr = dict(mean=p.sb("mean", [128, 1], F32), xc=p.sb("xc", [128, D], F32),
               sq=p.sb("sq", [128, D], F32), var=p.sb("var", [128, 1], F32),
               qs=p.sb("qs", [128, 1], F32))
    ident = p.sb("ident", [128, 128], BF16)
    psT = p.ps("psT", [128, 128], BF16)
    psTf = p.ps("psTf", [8, 128], BF16)
    wel = p.sb("wel", [128, D], F32)
    web = [[p.sb(f"web{e}_{k}", [128, D], BF16) for k in range(8)]
           for e in range(E_)]
    rwt = p.sb("rwt", [128, 64], BF16)
    rwl = p.sb("rwl", [128, 64], F32)
    ebt = p.sb("ebt", [8, D], BF16)
    ebl = p.sb("ebl", [8, D], F32)
    rbb = p.sb("rbb", [128, E_], F32)
    rbl = p.sb("rbl", [1, E_], F32)
    onesf = p.sb("onesf", [1, 128], F32)
    psB = p.ps("psB", [128, E_], F32)
    lg = p.sb("lg", [128, E_], F32)
    mx8 = p.sb("mx8", [128, 1], F32)
    meq = p.sb("meq", [128, 8], F32)
    mcum = p.sb("mcum", [128, 8], F32)
    zer8 = p.sb("zer8", [128, 8], F32)
    gcol = p.sb("gcol", [128, 1], F32)
    mcols = p.sb("mcols", [128, 8], F32)
    mcolsb = p.sb("mcolsb", [128, 8], BF16)
    mTb = p.sb("mTb", [8, 128], BF16)
    pr = p.ps("pr", [128, E_], F32)
    pm = [p.ps(f"pm{b}", [128, 512], F32) for b in range(2)]
    y = p.sb("y", [128, D], F32)
    rtile = p.sb("rtile", [128, D], F32)

    p.begin()
    p.mset(zer8[:], 0.0)
    p.dma(ident[:], identb[:], group="cst")
    p.mset(onesf[:], 1.0)
    p.dma(rwl[:].rearrange("p (k e) -> p k e", e=E_),
          router_w.rearrange("(k p) e -> p k e", p=128), group="cst")
    p.cpy(rwt[:], rwl[:])
    p.dma(rbl[:], router_b[:], group="cst")
    p.mm(psB[:], [(onesf[:], rbl[:])], name="bc_rb")
    p.cpy(rbb[:], psB[:])
    p.dma(ebl[:], expert_b.rearrange("e d -> e d"), group="cst")
    p.cpy(ebt[:], ebl[:])
    for e in range(E_):
        for k in range(8):
            p.dma(wel[:], expert_w[e, :, :].rearrange("(k p) d -> p k d", p=128)[:, k, :],
                  group="ldw")
            p.cpy(web[e][k][:], wel[:])

    for i in range(NT):
        x = xld[i % 2]
        p.dma(x[:], resid[i * 128:(i + 1) * 128, :], group=f"ldx{i % 2}")
        emit_ln(p, x[:], scr, xn[:], D)
        for j in range(D // 128):
            p.tr(psT[:], xn[:, j * 128:(j + 1) * 128], ident[:])
            p.cpy(xnT[j][:, i * 128:(i + 1) * 128], psT[:])

    for i in range(NT):
        tsl = slice(i * 128, (i + 1) * 128)
        p.mm(pr[:], [(xnT[k][:, tsl], rwt[:, k * 8:(k + 1) * 8]) for k in range(8)],
             name=f"rt{i}")
        p.cpy(lg[:], pr[:])
        p.tt(lg[:], lg[:], rbb[:], ALU.add)
        p.red(mx8[:, 0:1], lg[:], op=ALU.max, name="lmax")
        p.ts(meq[:], lg[:], mx8[:, 0:1], op0=ALU.is_equal)
        p.scan(mcum[:], meq[:], zer8[:], 0.0, name="cum8", op0=ALU.add)
        p.ts(mcum[:], mcum[:], 1.0, op0=ALU.is_equal)
        p.tt(meq[:], meq[:], mcum[:], ALU.mult)
        p.ts(lg[:], lg[:], mx8[:, 0:1], op0=ALU.subtract)
        p.act(lg[:], lg[:], AF.Exp)
        p.red(gcol[:], lg[:], op=ALU.add)
        p.rcp(gcol[:], gcol[:])
        p.ts(gcol[:], gcol[:], 0.5)
        p.ts(mcols[:], meq[:], gcol[:], op0=ALU.mult)
        p.cpy(mcolsb[:], mcols[:])
        p.em.emit("tensor", lambda eng: eng.transpose(
            out=psTf[:], in_=mcolsb[:], identity=ident[:]),
            reads=[mcolsb, ident], writes=[psTf], name="trm")
        p.cpy(mTb[:], psTf[:])
        p.dma(rtile[:], resid[tsl, :], group="ldr")
        for b in range(2):
            csl = slice(b * 512, (b + 1) * 512)
            p.mm(pm[b][:], [(mTb[:], ebt[:, csl])], name=f"bias{i}_{b}")
            p.stt(y[:, csl], pm[b][:], 1.0, rtile[:, csl], op0=ALU.mult,
                  op1=ALU.add)
        for e in range(E_):
            for b in range(2):
                csl = slice(b * 512, (b + 1) * 512)
                p.mm(pm[b][:], [(xnT[k][:, tsl], web[e][k][:, csl])
                                for k in range(8)], name=f"ex{i}_{e}_{b}")
                p.stt(y[:, csl], pm[b][:], mcols[:, e:e + 1], y[:, csl],
                      op0=ALU.mult, op1=ALU.add)
        p.dma(resid2[tsl, :], y[:, 0:D], group="st")
    nc = p.finish()
    return nc, p.outs




def build_attn_a():
    H, DH = 16, 64
    NT = TCORE // 128
    p = Prog()
    resid = p.inp("resid", [TCORE, D])
    w_qkv = p.inp("w_qkv", [3 * D, D])
    identb = p.inp("identb", [128, 128], BF16)
    e2 = p.inp("e2", [128, 2], BF16)
    e2t = p.inp("e2t", [2, 128], BF16)
    qhat = p.out("qhat", [D, TCORE], F32)
    kv_part = p.out("kv_part", [DH, H * DH])
    ksum_part = p.out("ksum_part", [1, H * DH])

    xld = [p.sb(f"xld{i}", [128, D], F32) for i in range(2)]
    xn = p.sb("xn", [128, D], BF16)
    xnT = [p.sb(f"xnT{j}", [128, TCORE], BF16) for j in range(D // 128)]
    scr = dict(mean=p.sb("mean", [128, 1], F32), xc=p.sb("xc", [128, D], F32),
               sq=p.sb("sq", [128, D], F32), var=p.sb("var", [128, 1], F32),
               qs=p.sb("qs", [128, 1], F32))
    ident = p.sb("ident", [128, 128], BF16)
    psT = p.ps("psT", [128, 128], BF16)
    wld = [p.sb(f"wld{i}", [128, D], F32) for i in range(2)]
    wlb = p.sb("wlb", [128, D], BF16)
    wqT = [p.sb(f"wqT{j}", [128, 3 * D], BF16) for j in range(D // 128)]
    e2s = p.sb("e2s", [128, 2], BF16)
    e2ts = p.sb("e2ts", [2, 128], BF16)
    pmo = p.ps("pmo", [128, 512], F32)
    psS = p.ps("psS", [2, 512], F32)
    psBC = p.ps("psBC", [128, 512], F32)
    rr = p.sb("rr", [128, 512], F32)
    sqs = p.sb("sqs", [128, 512], F32)
    r3b = p.sb("r3b", [128, 512], BF16)
    Ssm = p.sb("Ssm", [2, 512], F32)
    Ssb = p.sb("Ssb", [2, 512], BF16)
    khat = p.sb("khat", [128, 512], BF16)
    khatT = [p.sb(f"khatT{i}", [128, D], BF16) for i in range(NT)]
    vtok = [p.sb(f"vtok{i}", [128, D], BF16) for i in range(NT)]
    onesc = p.sb("onesc", [128, 1], BF16)
    kvp = p.sb("kvp", [64, H * DH], F32)
    kss = p.sb("kss", [1, H * DH], F32)
    pkv = [p.ps(f"pkv{g}", [64, 512], F32) for g in range(2)]
    pks = [p.ps(f"pks{b}", [1, 512], F32) for b in range(2)]

    p.begin()
    p.dma(ident[:], identb[:], group="cst")
    p.dma(e2s[:], e2[:], group="cst")
    p.dma(e2ts[:], e2t[:], group="cst")
    p.mset(onesc[:], 1.0)

    for i in range(NT):
        x = xld[i % 2]
        p.dma(x[:], resid[i * 128:(i + 1) * 128, :], group=f"ldx{i % 2}")
        emit_ln(p, x[:], scr, xn[:], D)
        for j in range(D // 128):
            p.tr(psT[:], xn[:, j * 128:(j + 1) * 128], ident[:])
            p.cpy(xnT[j][:, i * 128:(i + 1) * 128], psT[:])

    for o in range(3 * D // 128):
        wl = wld[o % 2]
        p.dma(wl[:], w_qkv[o * 128:(o + 1) * 128, :], group=f"ldw{o % 2}")
        p.cpy(wlb[:], wl[:])
        for j in range(D // 128):
            p.tr(psT[:], wlb[:, j * 128:(j + 1) * 128], ident[:])
            p.cpy(wqT[j][:, o * 128:(o + 1) * 128], psT[:])

    for ot in range(16):
        osl = slice(ot * 128, (ot + 1) * 128)
        for c in range(4):
            tsl = slice(c * 512, (c + 1) * 512)
            p.mm(pmo[:], [(wqT[k][:, osl], xnT[k][:, tsl]) for k in range(8)],
                 name=f"qk{ot}_{c}")
            p.ts(rr[:], pmo[:], 0.0, 1e-6, op0=ALU.max, op1=ALU.add)
            p.act(sqs[:], rr[:], AF.Square)
            p.tt(r3b[:], sqs[:], rr[:], ALU.mult)
            p.mm(psS[:], [(e2s[:], r3b[:])], name=f"S{ot}_{c}")
            p.cpy(Ssm[:], psS[:])
            p.rcp(Ssm[:], Ssm[:])
            p.cpy(Ssb[:], Ssm[:])
            p.mm(psBC[:], [(e2ts[:], Ssb[:])], name=f"bc{ot}_{c}")
            p.tt(khat[:], r3b[:], psBC[:], ALU.mult)
            if ot < 8:
                p.tt(rr[:], r3b[:], psBC[:], ALU.mult)
                p.dma(qhat[ot * 128:(ot + 1) * 128, tsl], rr[:], group="sq",
                      eng="gpsimd")
            else:
                for tb in range(4):
                    i = c * 4 + tb
                    p.tr(psT[:], khat[:, tb * 128:(tb + 1) * 128], ident[:])
                    p.cpy(khatT[i][:, (ot - 8) * 128:(ot - 7) * 128], psT[:])
    for i in range(NT):
        tsl = slice(i * 128, (i + 1) * 128)
        for b in range(2):
            vsl = slice(2 * D + b * 512, 2 * D + (b + 1) * 512)
            p.mm(pmo[:], [(xnT[k][:, tsl], wqT[k][:, vsl]) for k in range(8)],
                 name=f"v{i}_{b}")
            p.cpy(vtok[i][:, b * 512:(b + 1) * 512], pmo[:])

    for h in range(H):
        hs = slice(h * 64, (h + 1) * 64)
        pv = pkv[h // 8][:, (h % 8) * 64:(h % 8) * 64 + 64]
        p.mm(pv, [(khatT[i][:, hs], vtok[i][:, hs]) for i in range(NT)],
             name=f"kv{h}")
        p.cpy(kvp[:, hs], pv)
    p.dma(kv_part[:, :], kvp[:, :], group="skv")
    for b in range(2):
        p.mm(pks[b][:], [(onesc[:], khatT[i][:, b * 512:(b + 1) * 512])
                         for i in range(NT)], name=f"ks{b}")
        p.cpy(kss[:, b * 512:(b + 1) * 512], pks[b][:])
    p.dma(ksum_part[:, :], kss[:, :], group="skv")
    nc = p.finish()
    return nc, p.outs


def build_attn_b():
    H, DH = 16, 64
    TE = TCORE + 2
    NT = TCORE // 128
    p = Prog()
    resid = p.inp("resid", [TCORE, D])
    qh_in = p.inp("qh_in", [D, TE], F32)
    kvp4 = p.inp("kvp4", [4, DH, H * DH])
    ks4 = p.inp("ks4", [4, H * DH])
    w_proj = p.inp("w_proj", [D, D])
    b_proj = p.inp("b_proj", [1, D])
    w_dwc = p.inp("w_dwc", [D, 3])
    b_dwc = p.inp("b_dwc", [1, D])
    identb = p.inp("identb", [128, 128], BF16)
    resid2 = p.out("resid2", [TCORE, D])

    ident = p.sb("ident", [128, 128], BF16)
    psT = p.ps("psT", [128, 128], BF16)
    qhh = [p.sb(f"qhh{h}", [64, TE], BF16) for h in range(16)]
    qhl = p.sb("qhl", [64, TE], F32)
    kvl = [p.sb(f"kvl{c}", [64, H * DH], F32) for c in range(2)]
    kv = p.sb("kv", [64, H * DH], F32)
    kvb = p.sb("kvb", [64, H * DH], BF16)
    ksb4l = None
    onesc4 = p.sb("onesc4", [4, 1], BF16)
    ksb4 = p.sb("ksb4", [4, H * DH], BF16)
    pks = [p.ps(f"pksB{b}", [1, 512], F32) for b in range(2)]
    ksr = p.sb("ksr", [1, H * DH], F32)
    ksrb = p.sb("ksrb", [1, H * DH], BF16)
    ksC = p.sb("ksC", [64, 16], BF16)
    psZ = p.ps("psZ", [1, 512], F32)
    zrow = None
    zrb = p.sb("zrb", [1, 512], BF16)
    ones1 = p.sb("ones1", [1, 64], BF16)
    psZB = p.ps("psZB", [64, 512], F32)
    zbc = p.sb("zbc", [64, 512], BF16)
    pso = p.ps("pso", [64, 512], F32)
    attn = [p.sb(f"attn{j}", [128, TE], BF16) for j in range(D // 128)]
    a2 = [p.sb(f"a2{j}", [128, TCORE], BF16) for j in range(D // 128)]
    wdw = p.sb("wdw", [128, 3 * 8], F32)
    bdw = p.sb("bdw", [128, 8], BF16)
    bdwf = p.sb("bdwf", [128, 8], F32)
    bdlb = p.sb("bdlb", [1, D], BF16)
    cvt = p.sb("cvt", [128, TCORE], BF16)
    wpl = [p.sb(f"wpl{i}", [128, D], F32) for i in range(2)]
    wplb = p.sb("wplb", [128, D], BF16)
    wpT = [p.sb(f"wpT{j}", [128, D], BF16) for j in range(D // 128)]
    bpb = p.sb("bpb", [128, D], F32)
    bpl = p.sb("bpl", [1, D], F32)
    onesf = p.sb("onesf", [1, 128], F32)
    rtile = p.sb("rtile", [128, D], F32)
    pmm = p.ps("pmmB", [128, 512], F32)

    p.begin()
    p.dma(ident[:], identb[:], group="cst")
    p.mset(onesc4[:], 1.0)
    p.mset(ones1[:], 1.0)
    p.mset(onesf[:], 1.0)
    for h in range(16):
        p.dma(qhl[:], qh_in[h * 64:(h + 1) * 64, :], group="lq")
        p.cpy(qhh[h][:], qhl[:])
    ksb4l = kvl[1][0:4, :]
    p.dma(ksb4l, ks4[:, :], group="lkv2")
    p.cpy(ksb4[:], ksb4l)
    p.dma(kvl[0][:], kvp4[0, :, :], group="lkv")
    p.dma(kvl[1][:], kvp4[1, :, :], group="lkv")
    p.tt(kv[:], kvl[0][:], kvl[1][:], ALU.add)
    p.dma(kvl[0][:], kvp4[2, :, :], group="lkv")
    p.tt(kv[:], kv[:], kvl[0][:], ALU.add)
    p.dma(kvl[1][:], kvp4[3, :, :], group="lkv")
    p.tt(kv[:], kv[:], kvl[1][:], ALU.add)
    p.cpy(kvb[:], kv[:])
    for b in range(2):
        p.mm(pks[b][:], [(onesc4[:], ksb4[:, b * 512:(b + 1) * 512])],
             name=f"ksum{b}")
        p.cpy(ksr[:, b * 512:(b + 1) * 512], pks[b][:])
    p.cpy(ksrb[:], ksr[:])
    for h in range(16):
        p.tr(psT[0:64, 0:1], ksrb[:, h * 64:(h + 1) * 64], ident[0:1, 0:1])
        p.cpy(ksC[:, h:h + 1], psT[0:64, 0:1])

    zrow = kvl[0][0:1, 0:512]
    chunks = [(c * 512, min((c + 1) * 512, TE)) for c in range((TE + 511) // 512)]
    for h in range(H):
        j2, r0 = h // 2, (h % 2) * 64
        hs = slice(h * 64, (h + 1) * 64)
        ksum_h = ksC[:, h:h + 1]
        for (c0, c1) in chunks:
            w = c1 - c0
            qsl = qhh[h][:, c0:c1]
            p.mm(psZ[:, 0:w], [(ksum_h, qsl)], name=f"z{h}_{c0}")
            p.ts(zrow[:, 0:w], psZ[:, 0:w], 1.0, 1e-6, op0=ALU.mult, op1=ALU.add)
            p.rcp(zrow[:, 0:w], zrow[:, 0:w])
            p.cpy(zrb[:, 0:w], zrow[:, 0:w])
            p.mm(psZB[:, 0:w], [(ones1[:], zrb[:, 0:w])], name=f"zb{h}_{c0}")
            p.cpy(zbc[:, 0:w], psZB[:, 0:w])
            p.mm(pso[:, 0:w], [(kvb[:, hs], qsl)], name=f"o{h}_{c0}")
            p.tt(attn[j2][r0:r0 + 64, c0:c1], pso[:, 0:w], zbc[:, 0:w], ALU.mult)

    p.dma(wdw[:].rearrange("p (j k) -> p j k", k=3),
          w_dwc.rearrange("(j p) k -> p j k", p=128), group="cw")
    p.dma(bpl[:], b_dwc[:, :], group="cw")
    p.cpy(bdlb[:], bpl[:])
    for j in range(D // 128):
        p.tr(psT[:, 0:1], bdlb[:, j * 128:(j + 1) * 128], ident[0:1, 0:1])
        p.cpy(bdw[:, j:j + 1], psT[:, 0:1])
    p.cpy(bdwf[:], bdw[:])
    for j in range(D // 128):
        p.ts(cvt[:], attn[j][:, 0:TCORE], wdw[:, 3 * j:3 * j + 1],
             bdwf[:, j:j + 1], op0=ALU.mult, op1=ALU.add)
        p.stt(cvt[:], attn[j][:, 1:TCORE + 1], wdw[:, 3 * j + 1:3 * j + 2],
              cvt[:], op0=ALU.mult, op1=ALU.add)
        p.stt(cvt[:], attn[j][:, 2:TCORE + 2], wdw[:, 3 * j + 2:3 * j + 3],
              cvt[:], op0=ALU.mult, op1=ALU.add)
        p.tt(a2[j][:], attn[j][:, 1:TCORE + 1], cvt[:], ALU.add)

    for o in range(D // 128):
        wl = wpl[o % 2]
        p.dma(wl[:], w_proj[o * 128:(o + 1) * 128, :], group=f"lwp{o % 2}")
        p.cpy(wplb[:], wl[:])
        for j in range(D // 128):
            p.tr(psT[:], wplb[:, j * 128:(j + 1) * 128], ident[:])
            p.cpy(wpT[j][:, o * 128:(o + 1) * 128], psT[:])
    p.dma(bpl[:], b_proj[:], group="cw")
    for b in range(2):
        p.mm(pmm[:], [(onesf[:], bpl[:, b * 512:(b + 1) * 512])], name=f"bb{b}")
        p.cpy(bpb[:, b * 512:(b + 1) * 512], pmm[:])

    for i in range(NT):
        tsl = slice(i * 128, (i + 1) * 128)
        p.dma(rtile[:], resid[tsl, :], group="ldr")
        for b in range(2):
            csl = slice(b * 512, (b + 1) * 512)
            p.mm(pmm[:], [(a2[k][:, tsl], wpT[k][:, csl]) for k in range(8)],
                 name=f"pj{i}_{b}")
            p.stt(rtile[:, csl], pmm[:], 1.0, rtile[:, csl], op0=ALU.mult,
                  op1=ALU.add)
            p.tt(rtile[:, csl], rtile[:, csl], bpb[:, csl], ALU.add)
        p.dma(resid2[tsl, :], rtile[:, 0:D], group="st")
    nc = p.finish()
    return nc, p.outs




def build_mamba():
    IN2 = 2048
    DST = 16
    TL = HALO + TCORE
    NTL = TL // 128
    CH = [(c * 512, min((c + 1) * 512, TL)) for c in range((TL + 511) // 512)]
    p = Prog()
    resid_h = p.inp("resid_h", [TL, D])
    w_in = p.inp("w_in", [D, 2 * IN2])
    conv_w = p.inp("conv_w", [4, IN2])
    conv_b = p.inp("conv_b", [1, IN2])
    w_x = p.inp("w_x", [IN2, 96])
    w_dt = p.inp("w_dt", [64, IN2])
    b_dt = p.inp("b_dt", [1, IN2])
    a_log = p.inp("a_log", [IN2, DST])
    d_vec = p.inp("d_vec", [1, IN2])
    w_out = p.inp("w_out", [IN2, D])
    identb = p.inp("identb", [128, 128], BF16)
    identf = p.inp("identf", [128, 128], F32)
    sel16 = p.inp("sel16", [16, 16 * 128], BF16)
    resid2 = p.out("resid2", [TCORE, D])

    d_rawT = p.dram("d_rawT", [IN2, TL], BF16)
    d_gateT = p.dram("d_gateT", [IN2, TL], BF16)
    d_hT = p.dram("d_hT", [IN2, TL], BF16)
    d_dtT = p.dram("d_dtT", [IN2, TL], F32)
    d_y2T = p.dram("d_y2T", [IN2, TCORE], BF16)

    xld = [p.sb("xld0", [128, D], F32)]
    xn = p.sb("xn", [128, D], BF16)
    xnT = [p.sb(f"xnT{j}", [128, TL], BF16) for j in range(D // 128)]
    scr = dict(mean=p.sb("mean", [128, 1], F32), xc=p.sb("xc", [128, D], F32),
               sq=p.sb("sq", [128, D], F32), var=p.sb("var", [128, 1], F32),
               qs=p.sb("qs", [128, 1], F32))
    ident = p.sb("ident", [128, 128], BF16)
    idf = p.sb("idf", [128, 128], F32)
    psT = p.ps("psT", [128, 128], BF16)
    psTf = p.ps("psTf", [128, 128], F32)
    winb = [p.sb(f"winb{k}", [128, 128], BF16) for k in range(8)]
    winl = p.sb("winl", [128, D], F32)
    pmo = p.ps("pmo", [128, 512], F32)
    evb = p.sb("evb", [128, 512], BF16)
    asb = p.sb("asb", [128, 256], F32)
    cwT = p.sb("cwT", [128, 64], F32)
    cbT = p.sb("cbT", [128, 16], F32)
    bdtT = p.sb("bdtT", [128, 16], F32)
    ddT = p.sb("ddT", [128, 16], F32)
    wxs = p.sb("wxs", [128, 16 * 96], BF16)
    wdt = p.sb("wdt", [64, IN2], BF16)
    wdtl = p.sb("wdtl", [64, IN2], F32)
    ones1b = p.sb("ones1b", [1, 128], BF16)
    sel = p.sb("sel", [16, 16 * 128], BF16)
    rawl = p.sb("rawl", [128, TL], BF16)
    cvt = p.sb("cvt", [128, TL], F32)
    htw = p.sb("htw", [128, TL], BF16)
    hkl = [p.sb(f"hkl{k}", [128, 128], BF16) for k in range(16)]
    ssp = p.sb("ssp", [128, 96], F32)
    sspb = p.sb("sspb", [128, 96], BF16)
    tsT = p.sb("tsT", [64, TL], BF16)
    BT = p.sb("BT", [16, TL], BF16)
    CT = p.sb("CT", [16, TL], BF16)
    dtw = p.sb("dtw", [128, TL], F32)
    cwl = dtw[0:4, 0:IN2]
    cbl = dtw[0:1, 0:IN2]
    wxl = dtw[:, 0:16 * 96]
    Bbc8 = [p.sb(f"Bbc{d}", [128, TL], BF16) for d in range(8)]
    Cbc1 = p.sb("Cbc1", [128, TL], BF16)
    dtt = p.sb("dtt", [128, TL], F32)
    htt = p.sb("htt", [128, TL], BF16)
    gtt = p.sb("gtt", [128, TL], BF16)
    ut = None
    dA = p.sb("dA", [128, TL], BF16)
    dBu = p.sb("dBu", [128, TL], BF16)
    ssc = p.sb("ssc", [128, TL], BF16)
    y2 = p.sb("y2", [128, TL], BF16)
    ymt = None
    psy = [p.ps(f"psy{c}", [128, 512], F32) for c in range(5)]
    wob = None
    rtile = p.sb("rtile", [128, D], F32)

    p.begin()
    p.dma(ident[:], identb[:], group="cst")
    p.dma(idf[:], identf[:], group="cst")
    p.dma(sel[:], sel16[:, :], group="cst")
    p.mset(ones1b[:], 1.0)

    p.dma(asb[:].rearrange("p (hb d) -> p hb d", d=DST),
          a_log.rearrange("(hb p) d -> p hb d", p=128), group="cst")
    p.act(asb[:], asb[:], AF.Exp)
    p.ts(asb[:], asb[:], -1.0)
    p.dma(cwl, conv_w[:, :], group="cst")
    for cb in range(16):
        p.tr(psTf[0:128, 0:4], cwl[:, cb * 128:(cb + 1) * 128], idf[0:4, 0:4])
        p.cpy(cwT[:, cb * 4:(cb + 1) * 4], psTf[0:128, 0:4])
    p.dma(cbl, conv_b[:, :], group="cst")
    for cb in range(16):
        p.tr(psTf[0:128, 0:1], cbl[:, cb * 128:(cb + 1) * 128], idf[0:1, 0:1])
        p.cpy(cbT[:, cb:cb + 1], psTf[0:128, 0:1])
    p.dma(cbl, b_dt[:, :], group="cst2")
    for cb in range(16):
        p.tr(psTf[0:128, 0:1], cbl[:, cb * 128:(cb + 1) * 128], idf[0:1, 0:1])
        p.cpy(bdtT[:, cb:cb + 1], psTf[0:128, 0:1])
    p.dma(cbl, d_vec[:, :], group="cst3")
    for cb in range(16):
        p.tr(psTf[0:128, 0:1], cbl[:, cb * 128:(cb + 1) * 128], idf[0:1, 0:1])
        p.cpy(ddT[:, cb:cb + 1], psTf[0:128, 0:1])
    p.dma(wxl.rearrange("p (k o) -> p k o", o=96),
          w_x.rearrange("(k p) o -> p k o", p=128), group="cst")
    p.cpy(wxs[:], wxl)
    p.dma(wdtl[:], w_dt[:, :], group="cst")
    p.cpy(wdt[:], wdtl[:])

    for i in range(NTL):
        x = xld[0]
        p.dma(x[:], resid_h[i * 128:(i + 1) * 128, :], group="ldx0")
        emit_ln(p, x[:], scr, xn[:], D)
        for j in range(D // 128):
            p.tr(psT[:], xn[:, j * 128:(j + 1) * 128], ident[:])
            p.cpy(xnT[j][:, i * 128:(i + 1) * 128], psT[:])

    for o in range(32):
        for k in range(8):
            p.dma(winl[:, 0:128], w_in[k * 128:(k + 1) * 128, o * 128:(o + 1) * 128],
                  group="lwi")
            p.cpy(winb[k][:], winl[:, 0:128])
        dst = d_rawT if o < 16 else d_gateT
        row = (o % 16) * 128
        for (c0, c1) in CH:
            p.mm(pmo[:, 0:c1 - c0], [(winb[k][:], xnT[k][:, c0:c1])
                                     for k in range(8)], name=f"rg{o}_{c0}")
            p.cpy(evb[:, 0:c1 - c0], pmo[:, 0:c1 - c0])
            p.dma(dst[row:row + 128, c0:c1], evb[:, 0:c1 - c0], group="srg",
                  eng="gpsimd")

    for cb in range(16):
        p.dma(rawl[:], d_rawT[cb * 128:(cb + 1) * 128, :], group="lraw")
        p.ts(cvt[:], rawl[:], cwT[:, cb * 4 + 3:cb * 4 + 4], op0=ALU.mult)
        p.stt(cvt[:, 1:TL], rawl[:, 0:TL - 1], cwT[:, cb * 4 + 2:cb * 4 + 3],
              cvt[:, 1:TL], op0=ALU.mult, op1=ALU.add)
        p.stt(cvt[:, 2:TL], rawl[:, 0:TL - 2], cwT[:, cb * 4 + 1:cb * 4 + 2],
              cvt[:, 2:TL], op0=ALU.mult, op1=ALU.add)
        p.stt(cvt[:, 3:TL], rawl[:, 0:TL - 3], cwT[:, cb * 4 + 0:cb * 4 + 1],
              cvt[:, 3:TL], op0=ALU.mult, op1=ALU.add)
        p.act(htw[:], cvt[:], AF.Silu, bias=cbT[:, cb:cb + 1])
        p.dma(d_hT[cb * 128:(cb + 1) * 128, :], htw[:], group="sh",
              eng="gpsimd")

    for i in range(NTL):
        for k in range(16):
            p.dma(hkl[k][:], d_hT[k * 128:(k + 1) * 128, i * 128:(i + 1) * 128],
                  group=f"lh{k % 4}")
        p.mm(pmo[:, 0:96], [(hkl[k][:], wxs[:, k * 96:(k + 1) * 96])
                            for k in range(16)], name=f"sp{i}")
        p.cpy(ssp[:], pmo[:, 0:96])
        p.cpy(sspb[:], ssp[:])
        p.tr(psT[0:64, :], sspb[:, 0:64], ident[:])
        p.cpy(tsT[:, i * 128:(i + 1) * 128], psT[0:64, :])
        p.tr(psT[0:16, :], sspb[:, 64:80], ident[:])
        p.cpy(BT[:, i * 128:(i + 1) * 128], psT[0:16, :])
        p.tr(psT[0:16, :], sspb[:, 80:96], ident[:])
        p.cpy(CT[:, i * 128:(i + 1) * 128], psT[0:16, :])
    for hb in range(16):
        hsl = slice(hb * 128, (hb + 1) * 128)
        for (c0, c1) in CH:
            p.mm(pmo[:, 0:c1 - c0], [(wdt[:, hsl], tsT[:, c0:c1])],
                 name=f"dt{hb}_{c0}")
            p.ts(dtw[:, c0:c1], pmo[:, 0:c1 - c0], bdtT[:, hb:hb + 1],
                 op0=ALU.add)
        p.act(cvt[:], dtw[:], AF.Abs)
        p.act(cvt[:], cvt[:], AF.Exp, scale=-1.0)
        p.act(cvt[:], cvt[:], AF.Ln, bias=1.0)
        p.ts(dtw[:], dtw[:], 0.0, op0=ALU.max)
        p.tt(dtw[:], dtw[:], cvt[:], ALU.add)
        p.dma(d_dtT[hsl, :], dtw[:], group="sdt", eng="gpsimd")

    Bbc = [xnT[d] for d in range(8)] + Bbc8
    for d in range(DST):
        for (c0, c1) in CH:
            p.mm(pmo[:, 0:c1 - c0], [(sel[:, d * 128:(d + 1) * 128], BT[:, c0:c1])],
                 name=f"bcB{d}_{c0}")
            p.cpy(Bbc[d][:, c0:c1], pmo[:, 0:c1 - c0])

    ut = rawl
    ymt = htw
    for hb in range(16):
        hsl = slice(hb * 128, (hb + 1) * 128)
        p.dma(dtt[:], d_dtT[hsl, :], group="ldt")
        p.dma(htt[:], d_hT[hsl, :], group="lht")
        p.tt(ut[:], dtt[:], htt[:], ALU.mult)
        for d in range(DST):
            Cb = Cbc1
            for (c0, c1) in CH:
                p.mm(pmo[:, 0:c1 - c0], [(sel[:, d * 128:(d + 1) * 128], CT[:, c0:c1])],
                     name=f"bcC{hb}_{d}_{c0}")
                p.cpy(Cb[:, c0:c1], pmo[:, 0:c1 - c0], eng="scalar")
            p.act(dA[:], dtt[:], AF.Exp, scale=asb[:, hb * 16 + d:hb * 16 + d + 1])
            p.tt(dBu[:], ut[:], Bbc[d][:], ALU.mult, eng="gpsimd")
            p.scan(ssc[:], dA[:], dBu[:], 0.0, name=f"scan{hb}_{d}")
            p.tt(ymt[:], ssc[:], Cb[:], ALU.mult)
            for c, (c0, c1) in enumerate(CH):
                p.mm(psy[c][:, 0:c1 - c0], [(ident[:], ymt[:, c0:c1])],
                     name=f"ya{hb}_{d}_{c}", start=(d == 0), stop=(d == DST - 1))
        p.dma(gtt[:], d_gateT[hsl, :], group="lgt")
        p.act(gtt[:], gtt[:], AF.Silu)
        for c, (c0, c1) in enumerate(CH):
            p.stt(cvt[:, c0:c1], htt[:, c0:c1], ddT[:, hb:hb + 1],
                  psy[c][:, 0:c1 - c0], op0=ALU.mult, op1=ALU.add)
        p.tt(y2[:], cvt[:], gtt[:], ALU.mult)
        p.dma(d_y2T[hsl, :], y2[:, HALO:TL], group="sy2", eng="gpsimd")

    wob = [Bbc[k][:, 0:D] for k in range(8)] + [Bbc8[k][:, D:2 * D] for k in range(8)]
    for k in range(16):
        p.dma(winl[:, 0:D], w_out[k * 128:(k + 1) * 128, :], group="lwo")
        p.cpy(wob[k], winl[:, 0:D])
    for i in range(TCORE // 128):
        tsl = slice(i * 128, (i + 1) * 128)
        p.dma(rtile[:], resid_h[HALO + i * 128:HALO + (i + 1) * 128, :],
              group="ldr")
        for k in range(16):
            p.dma(hkl[k][:], d_y2T[k * 128:(k + 1) * 128, tsl], group=f"ly{k % 4}")
        for b in range(2):
            csl = slice(b * 512, (b + 1) * 512)
            p.mm(pmo[:], [(hkl[k][:], wob[k][:, csl]) for k in range(16)],
                 name=f"wo{i}_{b}")
            p.stt(rtile[:, csl], pmo[:], 1.0, rtile[:, csl], op0=ALU.mult,
                  op1=ALU.add)
        p.dma(resid2[tsl, :], rtile[:, 0:D], group="st")
    nc = p.finish()
    return nc, p.outs


_CACHE = {}


def _get(name):
    if name not in _CACHE:
        _CACHE[name] = dict(ffn=build_ffn, ffn2=build_ffn2, moe=build_moe, attn_a=build_attn_a,
                            attn_b=build_attn_b, mamba=build_mamba)[name]()
    return _CACHE[name]


def _consts():
    identb = np.eye(128).astype(BF)
    identf = np.eye(128, dtype=np.float32)
    sel16 = np.zeros((16, 16 * 128), BF)
    for d in range(16):
        sel16[d, d * 128:(d + 1) * 128] = 1
    e2 = np.zeros((128, 2), BF)
    e2[0:64, 0] = 1
    e2[64:128, 1] = 1
    e2t = np.ascontiguousarray(e2.T)
    return identb, identf, sel16, e2, e2t


def _launch(name, in_maps):
    nc, outs = _get(name)
    res = run_bass_kernel_spmd(nc, in_maps, core_ids=list(range(NCORES)))
    return [{k: np.asarray(r[k]) for k in outs} for r in res.results]


def kernel(L0_attn_x, L0_attn_W_in, L0_attn_conv_w, L0_attn_conv_b,
           L0_attn_W_x, L0_attn_W_dt, L0_attn_b_dt, L0_attn_A_log, L0_attn_D,
           L0_attn_W_out, L0_ffn_w_gate, L0_ffn_w_up, L0_ffn_w_down,
           L1_attn_w_qkv, L1_attn_w_proj, L1_attn_b_proj, L1_attn_w_dwc,
           L1_attn_b_dwc, L1_ffn_router_w, L1_ffn_router_b, L1_ffn_expert_w,
           L1_ffn_expert_b):
    identb, identf, sel16, e2, e2t = _consts()
    f32 = lambda a: np.ascontiguousarray(a, dtype=np.float32)
    x = f32(L0_attn_x).reshape(NCORES * TCORE, D)

    def shards(r):
        return [r[c * TCORE:(c + 1) * TCORE] for c in range(NCORES)]

    prep = {}

    def run_ffn(r):
        maps = [dict(resid=s, w_gate=f32(L0_ffn_w_gate), w_up=f32(L0_ffn_w_up),
                     w_down=f32(L0_ffn_w_down), identb=identb)
                for s in shards(r)]
        out = _launch("ffn", maps)
        prep.update(tgT=out[0]["o_tgT"], tuT=out[0]["o_tuT"],
                    tdT=out[0]["o_tdT"], wscal=out[0]["o_scal"])
        return np.concatenate([o["resid2"] for o in out])

    def run_ffn2(r):
        maps = [dict(resid=s, tgT=prep["tgT"], tuT=prep["tuT"],
                     tdT=prep["tdT"], wscal=prep["wscal"], identb=identb)
                for s in shards(r)]
        out = _launch("ffn2", maps)
        return np.concatenate([o["resid2"] for o in out])

    def run_mamba(r):
        zh = np.zeros((HALO, D), np.float32)
        maps = []
        cw = f32(L0_attn_conv_w).reshape(4, 2048)
        for c in range(NCORES):
            halo = zh if c % 4 == 0 else r[c * TCORE - HALO:c * TCORE]
            rh = np.concatenate([halo, r[c * TCORE:(c + 1) * TCORE]])
            maps.append(dict(
                resid_h=f32(rh), w_in=f32(L0_attn_W_in), conv_w=cw,
                conv_b=f32(L0_attn_conv_b).reshape(1, 2048),
                w_x=f32(L0_attn_W_x), w_dt=f32(L0_attn_W_dt),
                b_dt=f32(L0_attn_b_dt).reshape(1, 2048),
                a_log=f32(L0_attn_A_log),
                d_vec=f32(L0_attn_D).reshape(1, 2048),
                w_out=f32(L0_attn_W_out), identb=identb, identf=identf,
                sel16=sel16))
        out = _launch("mamba", maps)
        return np.concatenate([o["resid2"] for o in out])

    def run_moe(r):
        maps = [dict(resid=s, router_w=f32(L1_ffn_router_w),
                     router_b=f32(L1_ffn_router_b).reshape(1, 8),
                     expert_w=f32(L1_ffn_expert_w),
                     expert_b=f32(L1_ffn_expert_b), identb=identb)
                for s in shards(r)]
        out = _launch("moe", maps)
        return np.concatenate([o["resid2"] for o in out])

    def run_attn(r):
        mapsA = [dict(resid=s, w_qkv=f32(L1_attn_w_qkv), identb=identb,
                      e2=e2, e2t=e2t) for s in shards(r)]
        outA = _launch("attn_a", mapsA)
        qh = [o["qhat"] for o in outA]
        zc = np.zeros((D, 1), np.float32)
        mapsB = []
        for c in range(NCORES):
            g0 = (c // 4) * 4
            kvp4 = np.stack([outA[g0 + i]["kv_part"] for i in range(4)])
            ks4 = np.concatenate([outA[g0 + i]["ksum_part"] for i in range(4)])
            left = qh[c - 1][:, -1:] if c % 4 != 0 else zc
            right = qh[c + 1][:, 0:1] if c % 4 != 3 else zc
            qh_in = np.concatenate([left, qh[c], right], axis=1)
            mapsB.append(dict(resid=r[c * TCORE:(c + 1) * TCORE], qh_in=qh_in,
                              kvp4=kvp4, ks4=ks4, w_proj=f32(L1_attn_w_proj),
                              b_proj=f32(L1_attn_b_proj).reshape(1, D),
                              w_dwc=f32(L1_attn_w_dwc),
                              b_dwc=f32(L1_attn_b_dwc).reshape(1, D),
                              identb=identb))
        outB = _launch("attn_b", mapsB)
        return np.concatenate([o["resid2"] for o in outB])

    r = run_ffn(x)
    r = run_mamba(r)
    r = run_ffn2(r)
    r = run_moe(r)
    r = run_attn(r)
    r = run_moe(r)
    return r.reshape(2, 8192, D).astype(np.float32)


# version 13 (final)
# speedup vs baseline: 1.0216x; 1.0216x over previous
import contextlib
import numpy as np
import ml_dtypes
import concourse.bass as bass
import concourse.mybir as mybir
from concourse.bass_utils import run_bass_kernel_spmd

F32 = mybir.dt.float32
BF16 = mybir.dt.bfloat16
U32 = mybir.dt.uint32
AF = mybir.ActivationFunctionType
ALU = mybir.AluOpType
AX = mybir.AxisListType

ENGINES = ("sync", "scalar", "vector", "tensor", "gpsimd")
MAGIC = float(1.5 * 2 ** 23)
NCORES = 8
TCORE = 2048
HALO = 256
D = 1024
BF = ml_dtypes.bfloat16


class Emitter:

    def __init__(self, nc, serialize=False):
        self.nc = nc
        self.serialize = serialize
        self.steps = []
        self.eng_steps = {e: [] for e in ENGINES}
        self.last_writer = {}
        self.readers = {}
        self.eng_count = {e: 0 for e in ENGINES}
        self.grp_count = {}

    @staticmethod
    def _key(x):
        t = getattr(x, "tensor", None)
        if t is not None:
            return t.name
        n = getattr(x, "name", None)
        return n if n is not None else x

    def emit(self, engine, fn, reads=(), writes=(), name="", dma=False,
             group=None, extra_deps=(), no_selfdep=False):
        idx = len(self.steps)
        deps = set(extra_deps)
        if self.serialize and idx > 0:
            deps.add(idx - 1)
        rk = [self._key(x) for x in reads]
        wk = [self._key(x) for x in writes]
        for k in rk:
            if k in self.last_writer:
                deps.add(self.last_writer[k])
        for k in wk:
            if k in self.last_writer:
                deps.add(self.last_writer[k])
            for r in self.readers.get(k, ()):
                deps.add(r)
        for k in rk:
            self.readers.setdefault(k, []).append(idx)
        for k in wk:
            self.last_writer[k] = idx
            self.readers[k] = []
        if dma:
            grp = group or (engine + "_dma")
            self.grp_count[grp] = self.grp_count.get(grp, 0) + 16
            sem_val = None
        else:
            grp = None
            self.eng_count[engine] += 1
            sem_val = self.eng_count[engine]
        st = dict(engine=engine, fn=fn, deps=sorted(deps), name=name, dma=dma,
                  grp=grp, sem_val=sem_val, idx=idx, no_selfdep=no_selfdep)
        self.steps.append(st)
        self.eng_steps[engine].append(st)
        return idx

    def _resolve_waits(self):
        run = {}
        for st in self.steps:
            if st["dma"]:
                run[st["grp"]] = run.get(st["grp"], 0) + 16
                st["grp_val"] = run[st["grp"]]
        for st in self.steps:
            waits = {}
            if st["dma"] and st["grp_val"] > 16:
                waits[("g", st["grp"])] = st["grp_val"] - 16
            for d in st["deps"]:
                ds = self.steps[d]
                if ds["dma"]:
                    key = ("g", ds["grp"])
                    v = ds["grp_val"]
                else:
                    if ds["engine"] == st["engine"] and st["no_selfdep"]:
                        continue
                    key = ("e", ds["engine"])
                    v = ds["sem_val"]
                waits[key] = max(waits.get(key, 0), v)
            st["waits"] = waits

    def build(self, block, ctx):
        nc = self.nc
        self._resolve_waits()
        sems = {}
        for e in ENGINES:
            if any(not s["dma"] for s in self.eng_steps[e]):
                sems[("e", e)] = ctx.enter_context(nc.semaphore(f"sem_{e}"))
        for g in self.grp_count:
            sems[("g", g)] = ctx.enter_context(nc.semaphore(f"sem_{g}"))

        def make_body(e):
            def body(eng):
                for st in self.eng_steps[e]:
                    for key, v in sorted(st["waits"].items()):
                        eng.wait_ge(sems[key], v)
                    inst = st["fn"](eng)
                    if inst is None:
                        raise RuntimeError(f"step {st['name']}: fn returned None")
                    if st["dma"]:
                        inst.then_inc(sems[("g", st["grp"])], 16)
                    else:
                        inst.then_inc(sems[("e", e)], 1)
            return body

        for e in ENGINES:
            if self.eng_steps[e]:
                getattr(block, e)(make_body(e))
        return sems


class Prog:
    def __init__(self):
        self.nc = bass.Bass("TRN2", target_bir_lowering=False)
        self.em = Emitter(self.nc)
        self.ctx = contextlib.ExitStack()
        self.outs = []

    def inp(self, name, shape, dt=F32):
        return self.nc.dram_tensor(name, shape, dt, kind="ExternalInput").ap()

    def out(self, name, shape, dt=F32):
        t = self.nc.dram_tensor(name, shape, dt, kind="ExternalOutput").ap()
        self.outs.append(name)
        return t

    def dram(self, name, shape, dt):
        return self.nc.dram_tensor(name, shape, dt).ap()

    def sb(self, name, shape, dt):
        return self.ctx.enter_context(self.nc.sbuf_tensor(name, shape, dt))

    def ps(self, name, shape, dt=F32):
        return self.ctx.enter_context(self.nc.psum_tensor(name, shape, dt))

    def begin(self):
        self.block = self.ctx.enter_context(self.nc.Block())

    def finish(self):
        self.em.emit("sync", lambda e: e.nop(), name="final_barrier",
                     extra_deps=[s["idx"] for s in self.em.steps if s["dma"]])
        self.em.build(self.block, self.ctx)
        self.ctx.close()
        return self.nc

    def dma(self, dst, src, group="d", name="", eng="sync"):
        self.em.emit(eng, lambda e: e.dma_start(out=dst, in_=src),
                     reads=[src], writes=[dst], dma=True, group=group, name=name)

    def tt(self, out, a, b, op, name="", eng="vector"):
        self.em.emit(eng, lambda e: e.tensor_tensor(out=out, in0=a, in1=b, op=op),
                     reads=[a, b], writes=[out], name=name)

    def ts(self, out, a, s1, s2=None, op0=ALU.mult, op1=ALU.bypass, name="",
           eng="vector"):
        ins = [a] + [s for s in (s1, s2) if hasattr(s, "tensor")]
        if s2 is None:
            self.em.emit(eng, lambda e: e.tensor_scalar(
                out=out, in0=a, scalar1=s1, scalar2=None, op0=op0),
                reads=ins, writes=[out], name=name)
        else:
            self.em.emit(eng, lambda e: e.tensor_scalar(
                out=out, in0=a, scalar1=s1, scalar2=s2, op0=op0, op1=op1),
                reads=ins, writes=[out], name=name)

    def stt(self, out, in0, scalar, in1, op0, op1, name="", eng="vector"):
        ins = [in0, in1] + ([scalar] if hasattr(scalar, "tensor") else [])
        self.em.emit(eng, lambda e: e.scalar_tensor_tensor(
            out=out, in0=in0, scalar=scalar, in1=in1, op0=op0, op1=op1),
            reads=ins, writes=[out], name=name)

    def act(self, out, in_, func, bias=0.0, scale=1.0, accum_out=None, name=""):
        ins = [in_] + [s for s in (bias, scale) if hasattr(s, "tensor")]
        ws = [out] + ([accum_out] if accum_out is not None else [])
        self.em.emit("scalar", lambda e: e.activation(
            out=out, in_=in_, func=func, bias=bias, scale=scale,
            accum_out=accum_out), reads=ins, writes=ws, name=name)

    def red(self, out, in_, op=ALU.add, absval=None, name="", negate=None):
        self.em.emit("vector", lambda e: e.tensor_reduce(
            out=out, in_=in_, axis=AX.X, op=op, apply_absolute_value=absval,
            negate=negate), reads=[in_], writes=[out], name=name)

    def rcp(self, out, in_, name=""):
        self.em.emit("vector", lambda e: e.reciprocal(out=out, in_=in_),
                     reads=[in_], writes=[out], name=name)

    def cpy(self, out, in_, name="", eng="vector"):
        self.em.emit(eng, lambda e: e.tensor_copy(out=out, in_=in_),
                     reads=[in_], writes=[out], name=name)

    def mset(self, t, v, name=""):
        self.em.emit("vector", lambda e: e.memset(t, v), writes=[t], name=name)

    def mm(self, psum, pairs, name="", extra_reads=(), start=True, stop=True):
        def fn(e):
            last = None
            n = len(pairs)
            for k, (l, r) in enumerate(pairs):
                last = e.matmul(out=psum, lhsT=l, rhs=r,
                                start=(start and k == 0),
                                stop=(stop and k == n - 1))
            return last
        rd = [x for pr in pairs for x in pr] + list(extra_reads)
        self.em.emit("tensor", fn, reads=rd, writes=[psum], name=name)

    def tr(self, psum, in_, ident, name=""):
        self.em.emit("tensor", lambda e: e.transpose(
            out=psum, in_=in_, identity=ident),
            reads=[in_, ident], writes=[psum], name=name)

    def scan(self, out, d0, d1, init, name="", op0=ALU.mult, op1=ALU.add):
        ins = [d0, d1] + ([init] if hasattr(init, "tensor") else [])
        self.em.emit("vector", lambda e: e.tensor_tensor_scan(
            out=out, data0=d0, data1=d1, initial=init,
            op0=op0, op1=op1), reads=ins, writes=[out], name=name)


def emit_ln(p, x_ap, scr, out_ap, width, f32_out_ap=None):
    mean, xc, sq, var = scr["mean"], scr["xc"], scr["sq"], scr["var"]
    p.red(mean[:], x_ap, name="ln_sum")
    p.ts(mean[:], mean[:], 1.0 / width)
    p.ts(xc[:, 0:width], x_ap, mean[:], op0=ALU.subtract)
    p.act(sq[:, 0:width], xc[:, 0:width], AF.Square, accum_out=var[:])
    p.ts(var[:], var[:], 1.0 / width, 1e-5, op0=ALU.mult, op1=ALU.add)
    p.act(var[:], var[:], AF.Sqrt)
    p.rcp(var[:], var[:])
    p.ts(out_ap, xc[:, 0:width], var[:], op0=ALU.mult)
    if f32_out_ap is not None:
        p.ts(f32_out_ap, xc[:, 0:width], var[:], op0=ALU.mult)


def emit_actquant(p, xn_ap, scr, out_bf16_ap, sc_col, width, int_view,
                  back_s, frac_s):
    qs = scr["qs"]
    p.red(sc_col[:], xn_ap, absval=True, op=ALU.max, name="aq_amax")
    p.ts(sc_col[:], sc_col[:], 1e-5, op0=ALU.max)
    p.rcp(qs[:], sc_col[:])
    p.ts(qs[:], qs[:], 127.0)
    p.ts(xn_ap, xn_ap, qs[:], op0=ALU.mult)
    emit_round(p, xn_ap, out_bf16_ap, width, int_view, back_s, frac_s,
               127.0, -128.0)
    p.ts(sc_col[:], sc_col[:], 1.0 / 127.0)


def emit_round(p, xs_ap, out_ap, width, int_view, back_s, frac_s, cmax, cmin):
    iw = int_view.shape[1]
    for c0 in range(0, width, iw):
        c1 = min(c0 + iw, width)
        w = c1 - c0
        xs = xs_ap[:, c0:c1]
        iv = int_view[:, 0:w]
        bk = back_s[:, 0:w]
        fr = frac_s[:, 0:w]
        oc = out_ap[:, c0:c1]
        p.cpy(iv, xs)
        p.cpy(bk, iv)
        p.tt(fr, xs, bk, ALU.subtract)
        p.stt(bk, fr, 0.5, bk, op0=ALU.is_gt, op1=ALU.add)
        p.ts(oc, fr, -0.5, op0=ALU.is_lt)
        p.tt(oc, bk, oc, ALU.subtract)
        p.ts(oc, oc, cmax, cmin, op0=ALU.min, op1=ALU.max)


def emit_colsum128(p, col_ap, scr32, out11, name=""):
    ones = scr32["ones128"]
    ps11 = scr32["ps11"]
    p.mm(ps11[:], [(col_ap, ones[:, 0:1])], name=name or "colsum")
    p.cpy(out11, ps11[:])


def build_ffn():
    INNER = 4096
    NT = TCORE // 128
    p = Prog()
    resid = p.inp("resid", [TCORE, D])
    w_gate = p.inp("w_gate", [INNER, D])
    w_up = p.inp("w_up", [INNER, D])
    w_down = p.inp("w_down", [D, INNER])
    identb = p.inp("identb", [128, 128], BF16)
    resid2 = p.out("resid2", [TCORE, D])
    o_tgT = p.out("o_tgT", [D, INNER], BF16)
    o_tuT = p.out("o_tuT", [D, INNER], BF16)
    o_tdT = p.out("o_tdT", [INNER, D], BF16)
    o_scal = p.out("o_scal", [1, 4])
    d_gact = p.dram("d_gact", [TCORE, INNER], BF16)
    d_guqT = p.dram("d_guqT", [INNER, TCORE], BF16)

    xld = [p.sb(f"xld{i}", [128, D], F32) for i in range(2)]
    xq = p.sb("xq", [128, D], BF16)
    xqT = [p.sb(f"xqT{j}", [128, TCORE], BF16) for j in range(D // 128)]
    gu = p.sb("gu", [128, INNER], F32)
    i32s = p.sb("i32s", [128, 1024], mybir.dt.int32)
    scr = dict(mean=p.sb("mean", [128, 1], F32), xc=p.sb("xc", [128, D], F32),
               sq=gu, var=p.sb("var", [128, 1], F32),
               qs=p.sb("qs", [128, 1], F32))
    scr32 = dict(ones128=p.sb("ones128", [128, 1], F32),
                 ps11=p.ps("ps11", [1, 1], F32))
    sc_x = [p.sb(f"scx{i}", [128, 1], F32) for i in range(NT)]
    sc_gu = [p.sb(f"scgu{i}", [128, 1], F32) for i in range(NT)]
    ident = p.sb("ident", [128, 128], BF16)
    psT = p.ps("psT", [128, 128], BF16)
    psT2 = p.ps("psT2", [128, 128], BF16)
    psTr = [psT, psT2]
    wld = [p.sb("wld0", [128, INNER], F32)]
    wtb = p.sb("wtb", [128, INNER], BF16)
    wabs = p.sb("wabs", [128, 32], F32)
    wred = p.sb("wred", [128, 1], F32)
    wmean = p.sb("wmean", [1, 1], F32)
    wrecip = p.sb("wrecip", [1, 1], F32)
    wcol = p.sb("wcol", [128, 1], F32)
    mcol = p.sb("mcol", [128, 1], F32)
    scc = p.sb("scc", [128, 1], F32)
    onesf = p.sb("onesf", [1, 128], F32)
    psB = p.ps("psB", [128, 1], F32)
    wTb = [[p.sb(f"wTb{v}_{j}", [128, 512], BF16) for j in range(D // 128)]
           for v in range(2)]
    wTd = [p.sb(f"wTd{j}", [128, D], BF16) for j in range(INNER // 128)]
    pm = [p.ps(f"pm{b}", [128, 512], F32) for b in range(4)]
    gact = p.sb("gact", [128, INNER], BF16)
    uact = p.sb("uact", [128, INNER], BF16)
    guq = p.sb("guq", [128, INNER], BF16)
    guqs4 = [p.sb(f"guqs{v}", [128, 128], BF16) for v in range(4)]
    rtile = p.sb("rtile", [128, D], F32)
    ot = p.sb("ot", [128, D], F32)

    p.begin()
    p.dma(ident[:], identb[:], group="cst")
    p.mset(onesf[:], 1.0)
    p.mset(scr32["ones128"][:], 1.0)

    for i in range(NT):
        x = xld[i % 2]
        p.dma(x[:], resid[i * 128:(i + 1) * 128, :], group=f"ldx{i % 2}")
        emit_ln(p, x[:], scr, scr["xc"][:, 0:D], D)
        emit_actquant(p, scr["xc"][:, 0:D], scr, xq[:], sc_x[i], D,
                       i32s[:], rtile[:], ot[:])
        for j in range(D // 128):
            p.tr(psT[:], xq[:, j * 128:(j + 1) * 128], ident[:], name=f"trx{i}_{j}")
            p.cpy(xqT[j][:, i * 128:(i + 1) * 128], psT[:])

    def w_meanpass(w_dram, nrows, ncols, denom):
        ntile = nrows // 128
        for i in range(ntile):
            wl = wld[0]
            p.dma(wl[:, 0:ncols], w_dram[i * 128:(i + 1) * 128, :],
                  group="ldw0")
            p.red(wabs[:, i:i + 1], wl[:, 0:ncols], absval=True, op=ALU.add)
        p.red(wred[:], wabs[:, 0:ntile], op=ALU.add)
        emit_colsum128(p, wred[:], scr32, wmean[:, 0:1])
        p.ts(wmean[:, 0:1], wmean[:, 0:1], 1.0 / denom, 1e-5,
             op0=ALU.mult, op1=ALU.max)
        p.rcp(wrecip[:, 0:1], wmean[:, 0:1])
        p.mm(psB[:], [(onesf[:], wrecip[:, 0:1])], name="bc_wr")
        p.cpy(wcol[:], psB[:])
        p.mm(psB[:], [(onesf[:], wmean[:, 0:1])], name="bc_wm")
        p.cpy(mcol[:], psB[:])

    def w_tern(wl_ap, ncols):
        p.ts(gu[:, 0:ncols], wl_ap, wcol[:], op0=ALU.mult)
        emit_round(p, gu[:, 0:ncols], wtb[:, 0:ncols], ncols, i32s[:],
                   rtile[:], ot[:], 1.0, -1.0)

    d_uact = p.dram("d_uact", [TCORE, INNER], BF16)
    for wi, (which, wsrc, dst) in enumerate((("g", w_gate, d_gact),
                                             ("u", w_up, d_uact))):
        w_meanpass(wsrc, INNER, D, INNER * D)
        p.dma(o_scal[:, wi:wi + 1], wmean[:, 0:1], group="sscal")
        oT = o_tgT if which == "g" else o_tuT
        for b in range(8):
            wt = wTb[b % 2]
            for r in range(4):
                row0 = b * 512 + r * 128
                wl = wld[0]
                p.dma(wl[:, 0:D], wsrc[row0:row0 + 128, :], group="ldw0")
                w_tern(wl[:, 0:D], D)
                for j in range(D // 128):
                    pT = psTr[j % 2]
                    p.tr(pT[:], wtb[:, j * 128:(j + 1) * 128], ident[:])
                    p.cpy(wt[j][:, r * 128:(r + 1) * 128], pT[:])
            for j in range(D // 128):
                p.dma(oT[j * 128:(j + 1) * 128, b * 512:(b + 1) * 512],
                      wt[j][:], group="swt", eng="gpsimd")
            for i in range(NT):
                ps = pm[i % 4]
                p.mm(ps[:], [(xqT[k][:, i * 128:(i + 1) * 128], wt[k][:])
                             for k in range(8)], name=f"mm{which}{b}_{i}")
                p.tt(scc[:], sc_x[i][:], mcol[:], ALU.mult)
                fn = AF.Silu if which == "g" else AF.Copy
                stg = gact if i % 2 == 0 else uact
                p.act(stg[:, 0:512], ps[:], fn, scale=scc[:])
                p.dma(dst[i * 128:(i + 1) * 128, b * 512:(b + 1) * 512],
                      stg[:, 0:512], group=f"sga{i % 2}", eng="gpsimd")

    for i in range(NT):
        p.dma(gact[:, 0:INNER], d_gact[i * 128:(i + 1) * 128, :], group="lga",
              eng="gpsimd")
        p.dma(uact[:, 0:INNER], d_uact[i * 128:(i + 1) * 128, :], group="lua",
              eng="gpsimd")
        p.tt(gu[:, 0:INNER], gact[:, 0:INNER], uact[:, 0:INNER], ALU.mult)
        emit_actquant(p, gu[:, 0:INNER], scr, guq[:], sc_gu[i], INNER,
                      i32s[:], rtile[:], ot[:])
        for j in range(INNER // 128):
            pT = psTr[j % 2]
            gq = guqs4[j % 4]
            p.tr(pT[:], guq[:, j * 128:(j + 1) * 128], ident[:])
            p.cpy(gq[:], pT[:])
            p.dma(d_guqT[j * 128:(j + 1) * 128, i * 128:(i + 1) * 128],
                  gq[:], group=f"sgq{j % 2}", eng="gpsimd")

    w_meanpass(w_down, D, INNER, D * INNER)
    p.dma(o_scal[:, 2:3], wmean[:, 0:1], group="sscal")
    p.mset(scr["qs"][0:1, 0:1], 0.0)
    p.dma(o_scal[:, 3:4], scr["qs"][0:1, 0:1], group="sscal")
    for r in range(8):
        wl = wld[0]
        p.dma(wl[:, 0:INNER], w_down[r * 128:(r + 1) * 128, :], group="ldw0")
        w_tern(wl[:, 0:INNER], INNER)
        for j in range(INNER // 128):
            pT = psTr[j % 2]
            p.tr(pT[:], wtb[:, j * 128:(j + 1) * 128], ident[:])
            p.cpy(wTd[j][:, r * 128:(r + 1) * 128], pT[:])
    for j in range(INNER // 128):
        p.dma(o_tdT[j * 128:(j + 1) * 128, :], wTd[j][:], group="swt",
              eng="gpsimd")
    guqTl = [xqT[v // 4][:, (v % 4) * 128:(v % 4) * 128 + 128] for v in range(32)]
    for i in range(NT):
        p.dma(rtile[:], resid[i * 128:(i + 1) * 128, :], group="ldr")
        for k in range(INNER // 128):
            p.dma(guqTl[k], d_guqT[k * 128:(k + 1) * 128, i * 128:(i + 1) * 128],
                  group=f"lgq{k % 4}")
        for b in range(2):
            ps = pm[b % 4]
            p.mm(ps[:], [(guqTl[k], wTd[k][:, b * 512:(b + 1) * 512])
                         for k in range(32)], name=f"mmd{i}_{b}")
            p.tt(scc[:], sc_gu[i][:], mcol[:], ALU.mult)
            p.ts(scc[:], scc[:], 0.5)
            p.stt(ot[:, b * 512:(b + 1) * 512], ps[:], scc[:],
                  rtile[:, b * 512:(b + 1) * 512], op0=ALU.mult, op1=ALU.add)
        p.dma(resid2[i * 128:(i + 1) * 128, :], ot[:, 0:D], group="st")
    nc = p.finish()
    return nc, p.outs






def build_ffn2():
    INNER = 4096
    NT = TCORE // 128
    p = Prog()
    resid = p.inp("resid", [TCORE, D])
    tgT = p.inp("tgT", [D, INNER], BF16)
    tuT = p.inp("tuT", [D, INNER], BF16)
    tdT = p.inp("tdT", [INNER, D], BF16)
    wscal = p.inp("wscal", [1, 4])
    identb = p.inp("identb", [128, 128], BF16)
    resid2 = p.out("resid2", [TCORE, D])
    d_gact = p.dram("d_gact", [TCORE, INNER], BF16)
    d_uact = p.dram("d_uact", [TCORE, INNER], BF16)
    d_guqT = p.dram("d_guqT", [INNER, TCORE], BF16)

    xld = [p.sb(f"xld{i}", [128, D], F32) for i in range(2)]
    xq = p.sb("xq", [128, D], BF16)
    xqT = [p.sb(f"xqT{j}", [128, TCORE], BF16) for j in range(D // 128)]
    gu = p.sb("gu", [128, INNER], F32)
    i32s = p.sb("i32s", [128, 1024], mybir.dt.int32)
    scr = dict(mean=p.sb("mean", [128, 1], F32), xc=p.sb("xc", [128, D], F32),
               sq=gu, var=p.sb("var", [128, 1], F32),
               qs=p.sb("qs", [128, 1], F32))
    sc_x = [p.sb(f"scx{i}", [128, 1], F32) for i in range(NT)]
    sc_gu = [p.sb(f"scgu{i}", [128, 1], F32) for i in range(NT)]
    ident = p.sb("ident", [128, 128], BF16)
    psT = p.ps("psT", [128, 128], BF16)
    psT2 = p.ps("psT2", [128, 128], BF16)
    psTr = [psT, psT2]
    wsl = p.sb("wsl", [1, 4], F32)
    mcol = [p.sb(f"mcol{v}", [128, 1], F32) for v in range(3)]
    scc = p.sb("scc", [128, 1], F32)
    onesf = p.sb("onesf", [1, 128], F32)
    psB = p.ps("psB", [128, 1], F32)
    wTb = [[p.sb(f"wTb{v}_{j}", [128, 512], BF16) for j in range(D // 128)]
           for v in range(2)]
    wTd = [p.sb(f"wTd{j}", [128, D], BF16) for j in range(INNER // 128)]
    pm = [p.ps(f"pm{b}", [128, 512], F32) for b in range(4)]
    gact = p.sb("gact", [128, INNER], BF16)
    uact = p.sb("uact", [128, INNER], BF16)
    guq = p.sb("guq", [128, INNER], BF16)
    guqs4 = [p.sb(f"guqs{v}", [128, 128], BF16) for v in range(4)]
    rtile = p.sb("rtile", [128, D], F32)
    ot = p.sb("ot", [128, D], F32)
    guqTl = [xqT[v // 4][:, (v % 4) * 128:(v % 4) * 128 + 128] for v in range(32)]

    p.begin()
    p.dma(ident[:], identb[:], group="cst")
    p.mset(onesf[:], 1.0)
    p.dma(wsl[:], wscal[:, :], group="cst")
    for v in range(3):
        p.mm(psB[:], [(onesf[:], wsl[:, v:v + 1])], name=f"bcm{v}")
        p.cpy(mcol[v][:], psB[:])

    for i in range(NT):
        x = xld[i % 2]
        p.dma(x[:], resid[i * 128:(i + 1) * 128, :], group=f"ldx{i % 2}")
        emit_ln(p, x[:], scr, scr["xc"][:, 0:D], D)
        emit_actquant(p, scr["xc"][:, 0:D], scr, xq[:], sc_x[i], D,
                      i32s[:], rtile[:], ot[:])
        for j in range(D // 128):
            pT = psTr[j % 2]
            p.tr(pT[:], xq[:, j * 128:(j + 1) * 128], ident[:])
            p.cpy(xqT[j][:, i * 128:(i + 1) * 128], pT[:])

    for wi, (which, wT_dram, dst) in enumerate((("g", tgT, d_gact),
                                                ("u", tuT, d_uact))):
        for b in range(8):
            wt = wTb[b % 2]
            for j in range(D // 128):
                p.dma(wt[j][:], wT_dram[j * 128:(j + 1) * 128,
                                        b * 512:(b + 1) * 512],
                      group=f"lw{j % 2}")
            for i in range(NT):
                ps = pm[i % 4]
                p.mm(ps[:], [(xqT[k][:, i * 128:(i + 1) * 128], wt[k][:])
                             for k in range(8)], name=f"mm{which}{b}_{i}")
                p.tt(scc[:], sc_x[i][:], mcol[wi][:], ALU.mult)
                fn = AF.Silu if which == "g" else AF.Copy
                stg = gact if i % 2 == 0 else uact
                p.act(stg[:, 0:512], ps[:], fn, scale=scc[:])
                p.dma(dst[i * 128:(i + 1) * 128, b * 512:(b + 1) * 512],
                      stg[:, 0:512], group=f"sga{i % 2}", eng="gpsimd")

    for i in range(NT):
        p.dma(gact[:, 0:INNER], d_gact[i * 128:(i + 1) * 128, :], group="lga",
              eng="gpsimd")
        p.dma(uact[:, 0:INNER], d_uact[i * 128:(i + 1) * 128, :], group="lua",
              eng="gpsimd")
        p.tt(gu[:, 0:INNER], gact[:, 0:INNER], uact[:, 0:INNER], ALU.mult)
        emit_actquant(p, gu[:, 0:INNER], scr, guq[:], sc_gu[i], INNER,
                      i32s[:], rtile[:], ot[:])
        for j in range(INNER // 128):
            pT = psTr[j % 2]
            gq = guqs4[j % 4]
            p.tr(pT[:], guq[:, j * 128:(j + 1) * 128], ident[:])
            p.cpy(gq[:], pT[:])
            p.dma(d_guqT[j * 128:(j + 1) * 128, i * 128:(i + 1) * 128],
                  gq[:], group=f"sgq{j % 2}", eng="gpsimd")

    for j in range(INNER // 128):
        p.dma(wTd[j][:], tdT[j * 128:(j + 1) * 128, :], group=f"lwd{j % 2}")
    for i in range(NT):
        tsl = slice(i * 128, (i + 1) * 128)
        p.dma(rtile[:], resid[tsl, :], group="ldr")
        for k in range(INNER // 128):
            p.dma(guqTl[k], d_guqT[k * 128:(k + 1) * 128, i * 128:(i + 1) * 128],
                  group=f"lgq{k % 4}")
        for b in range(2):
            ps = pm[b % 4]
            p.mm(ps[:], [(guqTl[k], wTd[k][:, b * 512:(b + 1) * 512])
                         for k in range(32)], name=f"mmd{i}_{b}")
            p.tt(scc[:], sc_gu[i][:], mcol[2][:], ALU.mult)
            p.ts(scc[:], scc[:], 0.5)
            p.stt(ot[:, b * 512:(b + 1) * 512], ps[:], scc[:],
                  rtile[:, b * 512:(b + 1) * 512], op0=ALU.mult, op1=ALU.add)
        p.dma(resid2[tsl, :], ot[:, 0:D], group="st")
    nc = p.finish()
    return nc, p.outs


def build_moe():
    E_ = 8
    NT = TCORE // 128
    p = Prog()
    resid = p.inp("resid", [TCORE, D])
    router_w = p.inp("router_w", [D, E_])
    router_b = p.inp("router_b", [1, E_])
    expert_w = p.inp("expert_w", [E_, D, D])
    expert_b = p.inp("expert_b", [E_, D])
    identb = p.inp("identb", [128, 128], BF16)
    resid2 = p.out("resid2", [TCORE, D])

    xld = [p.sb(f"xld{i}", [128, D], F32) for i in range(2)]
    xn = p.sb("xn", [128, D], BF16)
    xnT = [p.sb(f"xnT{j}", [128, TCORE], BF16) for j in range(D // 128)]
    scr = dict(mean=p.sb("mean", [128, 1], F32), xc=p.sb("xc", [128, D], F32),
               sq=p.sb("sq", [128, D], F32), var=p.sb("var", [128, 1], F32),
               qs=p.sb("qs", [128, 1], F32))
    ident = p.sb("ident", [128, 128], BF16)
    psT = p.ps("psT", [128, 128], BF16)
    psTf = p.ps("psTf", [8, 128], BF16)
    wel = p.sb("wel", [128, D], F32)
    web = [[p.sb(f"web{e}_{k}", [128, D], BF16) for k in range(8)]
           for e in range(E_)]
    rwt = p.sb("rwt", [128, 64], BF16)
    rwl = p.sb("rwl", [128, 64], F32)
    ebt = p.sb("ebt", [8, D], BF16)
    ebl = p.sb("ebl", [8, D], F32)
    rbb = p.sb("rbb", [128, E_], F32)
    rbl = p.sb("rbl", [1, E_], F32)
    onesf = p.sb("onesf", [1, 128], F32)
    psB = p.ps("psB", [128, E_], F32)
    lg = p.sb("lg", [128, E_], F32)
    mx8 = p.sb("mx8", [128, 1], F32)
    meq = p.sb("meq", [128, 8], F32)
    mcum = p.sb("mcum", [128, 8], F32)
    zer8 = p.sb("zer8", [128, 8], F32)
    gcol = p.sb("gcol", [128, 1], F32)
    mcols = p.sb("mcols", [128, 8], F32)
    mcolsb = p.sb("mcolsb", [128, 8], BF16)
    mTb = p.sb("mTb", [8, 128], BF16)
    pr = p.ps("pr", [128, E_], F32)
    pm = [p.ps(f"pm{b}", [128, 512], F32) for b in range(2)]
    y = p.sb("y", [128, D], F32)
    rtile = p.sb("rtile", [128, D], F32)

    p.begin()
    p.mset(zer8[:], 0.0)
    p.dma(ident[:], identb[:], group="cst")
    p.mset(onesf[:], 1.0)
    p.dma(rwl[:].rearrange("p (k e) -> p k e", e=E_),
          router_w.rearrange("(k p) e -> p k e", p=128), group="cst")
    p.cpy(rwt[:], rwl[:])
    p.dma(rbl[:], router_b[:], group="cst")
    p.mm(psB[:], [(onesf[:], rbl[:])], name="bc_rb")
    p.cpy(rbb[:], psB[:])
    p.dma(ebl[:], expert_b.rearrange("e d -> e d"), group="cst")
    p.cpy(ebt[:], ebl[:])
    for e in range(E_):
        for k in range(8):
            p.dma(wel[:], expert_w[e, :, :].rearrange("(k p) d -> p k d", p=128)[:, k, :],
                  group="ldw")
            p.cpy(web[e][k][:], wel[:])

    for i in range(NT):
        x = xld[i % 2]
        p.dma(x[:], resid[i * 128:(i + 1) * 128, :], group=f"ldx{i % 2}")
        emit_ln(p, x[:], scr, xn[:], D)
        for j in range(D // 128):
            p.tr(psT[:], xn[:, j * 128:(j + 1) * 128], ident[:])
            p.cpy(xnT[j][:, i * 128:(i + 1) * 128], psT[:])

    for i in range(NT):
        tsl = slice(i * 128, (i + 1) * 128)
        p.mm(pr[:], [(xnT[k][:, tsl], rwt[:, k * 8:(k + 1) * 8]) for k in range(8)],
             name=f"rt{i}")
        p.cpy(lg[:], pr[:])
        p.tt(lg[:], lg[:], rbb[:], ALU.add)
        p.red(mx8[:, 0:1], lg[:], op=ALU.max, name="lmax")
        p.ts(meq[:], lg[:], mx8[:, 0:1], op0=ALU.is_equal)
        p.scan(mcum[:], meq[:], zer8[:], 0.0, name="cum8", op0=ALU.add)
        p.ts(mcum[:], mcum[:], 1.0, op0=ALU.is_equal)
        p.tt(meq[:], meq[:], mcum[:], ALU.mult)
        p.ts(lg[:], lg[:], mx8[:, 0:1], op0=ALU.subtract)
        p.act(lg[:], lg[:], AF.Exp)
        p.red(gcol[:], lg[:], op=ALU.add)
        p.rcp(gcol[:], gcol[:])
        p.ts(gcol[:], gcol[:], 0.5)
        p.ts(mcols[:], meq[:], gcol[:], op0=ALU.mult)
        p.cpy(mcolsb[:], mcols[:])
        p.em.emit("tensor", lambda eng: eng.transpose(
            out=psTf[:], in_=mcolsb[:], identity=ident[:]),
            reads=[mcolsb, ident], writes=[psTf], name="trm")
        p.cpy(mTb[:], psTf[:])
        p.dma(rtile[:], resid[tsl, :], group="ldr")
        for b in range(2):
            csl = slice(b * 512, (b + 1) * 512)
            p.mm(pm[b][:], [(mTb[:], ebt[:, csl])], name=f"bias{i}_{b}")
            p.stt(y[:, csl], pm[b][:], 1.0, rtile[:, csl], op0=ALU.mult,
                  op1=ALU.add)
        for e in range(E_):
            for b in range(2):
                csl = slice(b * 512, (b + 1) * 512)
                p.mm(pm[b][:], [(xnT[k][:, tsl], web[e][k][:, csl])
                                for k in range(8)], name=f"ex{i}_{e}_{b}")
                p.stt(y[:, csl], pm[b][:], mcols[:, e:e + 1], y[:, csl],
                      op0=ALU.mult, op1=ALU.add)
        p.dma(resid2[tsl, :], y[:, 0:D], group="st")
    nc = p.finish()
    return nc, p.outs




def build_attn_a():
    H, DH = 16, 64
    NT = TCORE // 128
    p = Prog()
    resid = p.inp("resid", [TCORE, D])
    w_qkv = p.inp("w_qkv", [3 * D, D])
    identb = p.inp("identb", [128, 128], BF16)
    e2 = p.inp("e2", [128, 2], BF16)
    e2t = p.inp("e2t", [2, 128], BF16)
    qhat = p.out("qhat", [D, TCORE], F32)
    kv_part = p.out("kv_part", [DH, H * DH])
    ksum_part = p.out("ksum_part", [1, H * DH])

    xld = [p.sb(f"xld{i}", [128, D], F32) for i in range(2)]
    xn = p.sb("xn", [128, D], BF16)
    xnT = [p.sb(f"xnT{j}", [128, TCORE], BF16) for j in range(D // 128)]
    scr = dict(mean=p.sb("mean", [128, 1], F32), xc=p.sb("xc", [128, D], F32),
               sq=p.sb("sq", [128, D], F32), var=p.sb("var", [128, 1], F32),
               qs=p.sb("qs", [128, 1], F32))
    ident = p.sb("ident", [128, 128], BF16)
    psT = p.ps("psT", [128, 128], BF16)
    wld = [p.sb(f"wld{i}", [128, D], F32) for i in range(2)]
    wlb = p.sb("wlb", [128, D], BF16)
    wqT = [p.sb(f"wqT{j}", [128, 3 * D], BF16) for j in range(D // 128)]
    e2s = p.sb("e2s", [128, 2], BF16)
    e2ts = p.sb("e2ts", [2, 128], BF16)
    pmo = p.ps("pmo", [128, 512], F32)
    psS = p.ps("psS", [2, 512], F32)
    psBC = p.ps("psBC", [128, 512], F32)
    rr = p.sb("rr", [128, 512], F32)
    sqs = p.sb("sqs", [128, 512], F32)
    r3b = p.sb("r3b", [128, 512], BF16)
    Ssm = p.sb("Ssm", [2, 512], F32)
    Ssb = p.sb("Ssb", [2, 512], BF16)
    khat = p.sb("khat", [128, 512], BF16)
    khatT = [p.sb(f"khatT{i}", [128, D], BF16) for i in range(NT)]
    vtok = [p.sb(f"vtok{i}", [128, D], BF16) for i in range(NT)]
    onesc = p.sb("onesc", [128, 1], BF16)
    kvp = p.sb("kvp", [64, H * DH], F32)
    kss = p.sb("kss", [1, H * DH], F32)
    pkv = [p.ps(f"pkv{g}", [64, 512], F32) for g in range(2)]
    pks = [p.ps(f"pks{b}", [1, 512], F32) for b in range(2)]

    p.begin()
    p.dma(ident[:], identb[:], group="cst")
    p.dma(e2s[:], e2[:], group="cst")
    p.dma(e2ts[:], e2t[:], group="cst")
    p.mset(onesc[:], 1.0)

    for i in range(NT):
        x = xld[i % 2]
        p.dma(x[:], resid[i * 128:(i + 1) * 128, :], group=f"ldx{i % 2}")
        emit_ln(p, x[:], scr, xn[:], D)
        for j in range(D // 128):
            p.tr(psT[:], xn[:, j * 128:(j + 1) * 128], ident[:])
            p.cpy(xnT[j][:, i * 128:(i + 1) * 128], psT[:])

    for o in range(3 * D // 128):
        wl = wld[o % 2]
        p.dma(wl[:], w_qkv[o * 128:(o + 1) * 128, :], group=f"ldw{o % 2}")
        p.cpy(wlb[:], wl[:])
        for j in range(D // 128):
            p.tr(psT[:], wlb[:, j * 128:(j + 1) * 128], ident[:])
            p.cpy(wqT[j][:, o * 128:(o + 1) * 128], psT[:])

    for ot in range(16):
        osl = slice(ot * 128, (ot + 1) * 128)
        for c in range(4):
            tsl = slice(c * 512, (c + 1) * 512)
            p.mm(pmo[:], [(wqT[k][:, osl], xnT[k][:, tsl]) for k in range(8)],
                 name=f"qk{ot}_{c}")
            p.ts(rr[:], pmo[:], 0.0, 1e-6, op0=ALU.max, op1=ALU.add)
            p.act(sqs[:], rr[:], AF.Square)
            p.tt(r3b[:], sqs[:], rr[:], ALU.mult)
            p.mm(psS[:], [(e2s[:], r3b[:])], name=f"S{ot}_{c}")
            p.cpy(Ssm[:], psS[:])
            p.rcp(Ssm[:], Ssm[:])
            p.cpy(Ssb[:], Ssm[:])
            p.mm(psBC[:], [(e2ts[:], Ssb[:])], name=f"bc{ot}_{c}")
            p.tt(khat[:], r3b[:], psBC[:], ALU.mult)
            if ot < 8:
                p.tt(rr[:], r3b[:], psBC[:], ALU.mult)
                p.dma(qhat[ot * 128:(ot + 1) * 128, tsl], rr[:], group="sq",
                      eng="gpsimd")
            else:
                for tb in range(4):
                    i = c * 4 + tb
                    p.tr(psT[:], khat[:, tb * 128:(tb + 1) * 128], ident[:])
                    p.cpy(khatT[i][:, (ot - 8) * 128:(ot - 7) * 128], psT[:])
    for i in range(NT):
        tsl = slice(i * 128, (i + 1) * 128)
        for b in range(2):
            vsl = slice(2 * D + b * 512, 2 * D + (b + 1) * 512)
            p.mm(pmo[:], [(xnT[k][:, tsl], wqT[k][:, vsl]) for k in range(8)],
                 name=f"v{i}_{b}")
            p.cpy(vtok[i][:, b * 512:(b + 1) * 512], pmo[:])

    for h in range(H):
        hs = slice(h * 64, (h + 1) * 64)
        pv = pkv[h // 8][:, (h % 8) * 64:(h % 8) * 64 + 64]
        p.mm(pv, [(khatT[i][:, hs], vtok[i][:, hs]) for i in range(NT)],
             name=f"kv{h}")
        p.cpy(kvp[:, hs], pv)
    p.dma(kv_part[:, :], kvp[:, :], group="skv")
    for b in range(2):
        p.mm(pks[b][:], [(onesc[:], khatT[i][:, b * 512:(b + 1) * 512])
                         for i in range(NT)], name=f"ks{b}")
        p.cpy(kss[:, b * 512:(b + 1) * 512], pks[b][:])
    p.dma(ksum_part[:, :], kss[:, :], group="skv")
    nc = p.finish()
    return nc, p.outs


def build_attn_b():
    H, DH = 16, 64
    TE = TCORE + 2
    NT = TCORE // 128
    p = Prog()
    resid = p.inp("resid", [TCORE, D])
    qh_in = p.inp("qh_in", [D, TE], F32)
    kvp4 = p.inp("kvp4", [4, DH, H * DH])
    ks4 = p.inp("ks4", [4, H * DH])
    w_proj = p.inp("w_proj", [D, D])
    b_proj = p.inp("b_proj", [1, D])
    w_dwc = p.inp("w_dwc", [D, 3])
    b_dwc = p.inp("b_dwc", [1, D])
    identb = p.inp("identb", [128, 128], BF16)
    resid2 = p.out("resid2", [TCORE, D])

    ident = p.sb("ident", [128, 128], BF16)
    psT = p.ps("psT", [128, 128], BF16)
    qhh = [p.sb(f"qhh{h}", [64, TE], BF16) for h in range(16)]
    qhl = p.sb("qhl", [64, TE], F32)
    kvl = [p.sb(f"kvl{c}", [64, H * DH], F32) for c in range(2)]
    kv = p.sb("kv", [64, H * DH], F32)
    kvb = p.sb("kvb", [64, H * DH], BF16)
    ksb4l = None
    onesc4 = p.sb("onesc4", [4, 1], BF16)
    ksb4 = p.sb("ksb4", [4, H * DH], BF16)
    pks = [p.ps(f"pksB{b}", [1, 512], F32) for b in range(2)]
    ksr = p.sb("ksr", [1, H * DH], F32)
    ksrb = p.sb("ksrb", [1, H * DH], BF16)
    ksC = p.sb("ksC", [64, 16], BF16)
    psZ = p.ps("psZ", [1, 512], F32)
    zrow = None
    zrb = p.sb("zrb", [1, 512], BF16)
    ones1 = p.sb("ones1", [1, 64], BF16)
    psZB = p.ps("psZB", [64, 512], F32)
    zbc = p.sb("zbc", [64, 512], BF16)
    pso = p.ps("pso", [64, 512], F32)
    attn = [p.sb(f"attn{j}", [128, TE], BF16) for j in range(D // 128)]
    a2 = [p.sb(f"a2{j}", [128, TCORE], BF16) for j in range(D // 128)]
    wdw = p.sb("wdw", [128, 3 * 8], F32)
    bdw = p.sb("bdw", [128, 8], BF16)
    bdwf = p.sb("bdwf", [128, 8], F32)
    bdlb = p.sb("bdlb", [1, D], BF16)
    cvt = p.sb("cvt", [128, TCORE], BF16)
    wpl = [p.sb(f"wpl{i}", [128, D], F32) for i in range(2)]
    wplb = p.sb("wplb", [128, D], BF16)
    wpT = [p.sb(f"wpT{j}", [128, D], BF16) for j in range(D // 128)]
    bpb = p.sb("bpb", [128, D], F32)
    bpl = p.sb("bpl", [1, D], F32)
    onesf = p.sb("onesf", [1, 128], F32)
    rtile = p.sb("rtile", [128, D], F32)
    pmm = p.ps("pmmB", [128, 512], F32)

    p.begin()
    p.dma(ident[:], identb[:], group="cst")
    p.mset(onesc4[:], 1.0)
    p.mset(ones1[:], 1.0)
    p.mset(onesf[:], 1.0)
    for h in range(16):
        p.dma(qhl[:], qh_in[h * 64:(h + 1) * 64, :], group="lq")
        p.cpy(qhh[h][:], qhl[:])
    ksb4l = kvl[1][0:4, :]
    p.dma(ksb4l, ks4[:, :], group="lkv2")
    p.cpy(ksb4[:], ksb4l)
    p.dma(kvl[0][:], kvp4[0, :, :], group="lkv")
    p.dma(kvl[1][:], kvp4[1, :, :], group="lkv")
    p.tt(kv[:], kvl[0][:], kvl[1][:], ALU.add)
    p.dma(kvl[0][:], kvp4[2, :, :], group="lkv")
    p.tt(kv[:], kv[:], kvl[0][:], ALU.add)
    p.dma(kvl[1][:], kvp4[3, :, :], group="lkv")
    p.tt(kv[:], kv[:], kvl[1][:], ALU.add)
    p.cpy(kvb[:], kv[:])
    for b in range(2):
        p.mm(pks[b][:], [(onesc4[:], ksb4[:, b * 512:(b + 1) * 512])],
             name=f"ksum{b}")
        p.cpy(ksr[:, b * 512:(b + 1) * 512], pks[b][:])
    p.cpy(ksrb[:], ksr[:])
    for h in range(16):
        p.tr(psT[0:64, 0:1], ksrb[:, h * 64:(h + 1) * 64], ident[0:1, 0:1])
        p.cpy(ksC[:, h:h + 1], psT[0:64, 0:1])

    zrow = kvl[0][0:1, 0:512]
    chunks = [(c * 512, min((c + 1) * 512, TE)) for c in range((TE + 511) // 512)]
    for h in range(H):
        j2, r0 = h // 2, (h % 2) * 64
        hs = slice(h * 64, (h + 1) * 64)
        ksum_h = ksC[:, h:h + 1]
        for (c0, c1) in chunks:
            w = c1 - c0
            qsl = qhh[h][:, c0:c1]
            p.mm(psZ[:, 0:w], [(ksum_h, qsl)], name=f"z{h}_{c0}")
            p.ts(zrow[:, 0:w], psZ[:, 0:w], 1.0, 1e-6, op0=ALU.mult, op1=ALU.add)
            p.rcp(zrow[:, 0:w], zrow[:, 0:w])
            p.cpy(zrb[:, 0:w], zrow[:, 0:w])
            p.mm(psZB[:, 0:w], [(ones1[:], zrb[:, 0:w])], name=f"zb{h}_{c0}")
            p.cpy(zbc[:, 0:w], psZB[:, 0:w])
            p.mm(pso[:, 0:w], [(kvb[:, hs], qsl)], name=f"o{h}_{c0}")
            p.tt(attn[j2][r0:r0 + 64, c0:c1], pso[:, 0:w], zbc[:, 0:w], ALU.mult)

    p.dma(wdw[:].rearrange("p (j k) -> p j k", k=3),
          w_dwc.rearrange("(j p) k -> p j k", p=128), group="cw")
    p.dma(bpl[:], b_dwc[:, :], group="cw")
    p.cpy(bdlb[:], bpl[:])
    for j in range(D // 128):
        p.tr(psT[:, 0:1], bdlb[:, j * 128:(j + 1) * 128], ident[0:1, 0:1])
        p.cpy(bdw[:, j:j + 1], psT[:, 0:1])
    p.cpy(bdwf[:], bdw[:])
    for j in range(D // 128):
        p.ts(cvt[:], attn[j][:, 0:TCORE], wdw[:, 3 * j:3 * j + 1],
             bdwf[:, j:j + 1], op0=ALU.mult, op1=ALU.add)
        p.stt(cvt[:], attn[j][:, 1:TCORE + 1], wdw[:, 3 * j + 1:3 * j + 2],
              cvt[:], op0=ALU.mult, op1=ALU.add)
        p.stt(cvt[:], attn[j][:, 2:TCORE + 2], wdw[:, 3 * j + 2:3 * j + 3],
              cvt[:], op0=ALU.mult, op1=ALU.add)
        p.tt(a2[j][:], attn[j][:, 1:TCORE + 1], cvt[:], ALU.add)

    for o in range(D // 128):
        wl = wpl[o % 2]
        p.dma(wl[:], w_proj[o * 128:(o + 1) * 128, :], group=f"lwp{o % 2}")
        p.cpy(wplb[:], wl[:])
        for j in range(D // 128):
            p.tr(psT[:], wplb[:, j * 128:(j + 1) * 128], ident[:])
            p.cpy(wpT[j][:, o * 128:(o + 1) * 128], psT[:])
    p.dma(bpl[:], b_proj[:], group="cw")
    for b in range(2):
        p.mm(pmm[:], [(onesf[:], bpl[:, b * 512:(b + 1) * 512])], name=f"bb{b}")
        p.cpy(bpb[:, b * 512:(b + 1) * 512], pmm[:])

    for i in range(NT):
        tsl = slice(i * 128, (i + 1) * 128)
        p.dma(rtile[:], resid[tsl, :], group="ldr")
        for b in range(2):
            csl = slice(b * 512, (b + 1) * 512)
            p.mm(pmm[:], [(a2[k][:, tsl], wpT[k][:, csl]) for k in range(8)],
                 name=f"pj{i}_{b}")
            p.stt(rtile[:, csl], pmm[:], 1.0, rtile[:, csl], op0=ALU.mult,
                  op1=ALU.add)
            p.tt(rtile[:, csl], rtile[:, csl], bpb[:, csl], ALU.add)
        p.dma(resid2[tsl, :], rtile[:, 0:D], group="st")
    nc = p.finish()
    return nc, p.outs




def build_mamba():
    IN2 = 2048
    DST = 16
    TL = HALO + TCORE
    NTL = TL // 128
    CH = [(c * 512, min((c + 1) * 512, TL)) for c in range((TL + 511) // 512)]
    p = Prog()
    resid_h = p.inp("resid_h", [TL, D])
    w_in = p.inp("w_in", [D, 2 * IN2])
    conv_w = p.inp("conv_w", [4, IN2])
    conv_b = p.inp("conv_b", [1, IN2])
    w_x = p.inp("w_x", [IN2, 96])
    w_dt = p.inp("w_dt", [64, IN2])
    b_dt = p.inp("b_dt", [1, IN2])
    a_log = p.inp("a_log", [IN2, DST])
    d_vec = p.inp("d_vec", [1, IN2])
    w_out = p.inp("w_out", [IN2, D])
    identb = p.inp("identb", [128, 128], BF16)
    identf = p.inp("identf", [128, 128], F32)
    sel16 = p.inp("sel16", [16, 16 * 128], BF16)
    resid2 = p.out("resid2", [TCORE, D])

    d_rawT = p.dram("d_rawT", [IN2, TL], BF16)
    d_gateT = p.dram("d_gateT", [IN2, TL], BF16)
    d_hT = p.dram("d_hT", [IN2, TL], BF16)
    d_dtT = p.dram("d_dtT", [IN2, TL], F32)
    d_y2T = p.dram("d_y2T", [IN2, TCORE], BF16)

    xld = [p.sb("xld0", [128, D], F32)]
    xn = p.sb("xn", [128, D], BF16)
    xnT = [p.sb(f"xnT{j}", [128, TL], BF16) for j in range(D // 128)]
    scr = dict(mean=p.sb("mean", [128, 1], F32), xc=p.sb("xc", [128, D], F32),
               sq=p.sb("sq", [128, D], F32), var=p.sb("var", [128, 1], F32),
               qs=p.sb("qs", [128, 1], F32))
    ident = p.sb("ident", [128, 128], BF16)
    idf = p.sb("idf", [128, 128], F32)
    psT = p.ps("psT", [128, 128], BF16)
    psTf = p.ps("psTf", [128, 128], F32)
    winb = [p.sb(f"winb{k}", [128, 128], BF16) for k in range(8)]
    winl = p.sb("winl", [128, D], F32)
    pmo = p.ps("pmo", [128, 512], F32)
    evb = p.sb("evb", [128, 512], BF16)
    asb = p.sb("asb", [128, 256], F32)
    cwT = p.sb("cwT", [128, 64], F32)
    cbT = p.sb("cbT", [128, 16], F32)
    bdtT = p.sb("bdtT", [128, 16], F32)
    ddT = p.sb("ddT", [128, 16], F32)
    wxs = p.sb("wxs", [128, 16 * 96], BF16)
    wdt = p.sb("wdt", [64, IN2], BF16)
    wdtl = p.sb("wdtl", [64, IN2], F32)
    ones1b = p.sb("ones1b", [1, 128], BF16)
    sel = p.sb("sel", [16, 16 * 128], BF16)
    rawl = p.sb("rawl", [128, TL], BF16)
    cvt = p.sb("cvt", [128, TL], F32)
    htw = p.sb("htw", [128, TL], BF16)
    hkl = [p.sb(f"hkl{k}", [128, 128], BF16) for k in range(16)]
    ssp = p.sb("ssp", [128, 96], F32)
    sspb = p.sb("sspb", [128, 96], BF16)
    tsT = p.sb("tsT", [64, TL], BF16)
    BT = p.sb("BT", [16, TL], BF16)
    CT = p.sb("CT", [16, TL], BF16)
    dtw = p.sb("dtw", [128, TL], F32)
    cwl = dtw[0:4, 0:IN2]
    cbl = dtw[0:1, 0:IN2]
    wxl = dtw[:, 0:16 * 96]
    Bbc8 = [p.sb(f"Bbc{d}", [128, TL], BF16) for d in range(8)]
    Cbc1 = p.sb("Cbc1", [128, TL], BF16)
    dtt = p.sb("dtt", [128, TL], F32)
    htt = p.sb("htt", [128, TL], BF16)
    gtt = p.sb("gtt", [128, TL], BF16)
    ut = None
    dA = p.sb("dA", [128, TL], BF16)
    dBu = p.sb("dBu", [128, TL], BF16)
    ssc = p.sb("ssc", [128, TL], BF16)
    y2 = p.sb("y2", [128, TL], BF16)
    ymt = None
    psy = [p.ps(f"psy{c}", [128, 512], F32) for c in range(5)]
    wob = None
    rtile = p.sb("rtile", [128, D], F32)

    p.begin()
    p.dma(ident[:], identb[:], group="cst")
    p.dma(idf[:], identf[:], group="cst")
    p.dma(sel[:], sel16[:, :], group="cst")
    p.mset(ones1b[:], 1.0)

    p.dma(asb[:].rearrange("p (hb d) -> p hb d", d=DST),
          a_log.rearrange("(hb p) d -> p hb d", p=128), group="cst")
    p.act(asb[:], asb[:], AF.Exp)
    p.ts(asb[:], asb[:], -1.0)
    p.dma(cwl, conv_w[:, :], group="cst")
    for cb in range(16):
        p.tr(psTf[0:128, 0:4], cwl[:, cb * 128:(cb + 1) * 128], idf[0:4, 0:4])
        p.cpy(cwT[:, cb * 4:(cb + 1) * 4], psTf[0:128, 0:4])
    p.dma(cbl, conv_b[:, :], group="cst")
    for cb in range(16):
        p.tr(psTf[0:128, 0:1], cbl[:, cb * 128:(cb + 1) * 128], idf[0:1, 0:1])
        p.cpy(cbT[:, cb:cb + 1], psTf[0:128, 0:1])
    p.dma(cbl, b_dt[:, :], group="cst2")
    for cb in range(16):
        p.tr(psTf[0:128, 0:1], cbl[:, cb * 128:(cb + 1) * 128], idf[0:1, 0:1])
        p.cpy(bdtT[:, cb:cb + 1], psTf[0:128, 0:1])
    p.dma(cbl, d_vec[:, :], group="cst3")
    for cb in range(16):
        p.tr(psTf[0:128, 0:1], cbl[:, cb * 128:(cb + 1) * 128], idf[0:1, 0:1])
        p.cpy(ddT[:, cb:cb + 1], psTf[0:128, 0:1])
    p.dma(wxl.rearrange("p (k o) -> p k o", o=96),
          w_x.rearrange("(k p) o -> p k o", p=128), group="cst")
    p.cpy(wxs[:], wxl)
    p.dma(wdtl[:], w_dt[:, :], group="cst")
    p.cpy(wdt[:], wdtl[:])

    for i in range(NTL):
        x = xld[0]
        p.dma(x[:], resid_h[i * 128:(i + 1) * 128, :], group="ldx0")
        emit_ln(p, x[:], scr, xn[:], D)
        for j in range(D // 128):
            p.tr(psT[:], xn[:, j * 128:(j + 1) * 128], ident[:])
            p.cpy(xnT[j][:, i * 128:(i + 1) * 128], psT[:])

    for o in range(32):
        for k in range(8):
            p.dma(winl[:, 0:128], w_in[k * 128:(k + 1) * 128, o * 128:(o + 1) * 128],
                  group="lwi")
            p.cpy(winb[k][:], winl[:, 0:128])
        dst = d_rawT if o < 16 else d_gateT
        row = (o % 16) * 128
        for (c0, c1) in CH:
            p.mm(pmo[:, 0:c1 - c0], [(winb[k][:], xnT[k][:, c0:c1])
                                     for k in range(8)], name=f"rg{o}_{c0}")
            p.cpy(evb[:, 0:c1 - c0], pmo[:, 0:c1 - c0])
            p.dma(dst[row:row + 128, c0:c1], evb[:, 0:c1 - c0], group="srg",
                  eng="gpsimd")

    for cb in range(16):
        p.dma(rawl[:], d_rawT[cb * 128:(cb + 1) * 128, :], group="lraw")
        p.ts(cvt[:], rawl[:], cwT[:, cb * 4 + 3:cb * 4 + 4], op0=ALU.mult)
        p.stt(cvt[:, 1:TL], rawl[:, 0:TL - 1], cwT[:, cb * 4 + 2:cb * 4 + 3],
              cvt[:, 1:TL], op0=ALU.mult, op1=ALU.add)
        p.stt(cvt[:, 2:TL], rawl[:, 0:TL - 2], cwT[:, cb * 4 + 1:cb * 4 + 2],
              cvt[:, 2:TL], op0=ALU.mult, op1=ALU.add)
        p.stt(cvt[:, 3:TL], rawl[:, 0:TL - 3], cwT[:, cb * 4 + 0:cb * 4 + 1],
              cvt[:, 3:TL], op0=ALU.mult, op1=ALU.add)
        p.act(htw[:], cvt[:], AF.Silu, bias=cbT[:, cb:cb + 1])
        p.dma(d_hT[cb * 128:(cb + 1) * 128, :], htw[:], group="sh",
              eng="gpsimd")

    for i in range(NTL):
        for k in range(16):
            p.dma(hkl[k][:], d_hT[k * 128:(k + 1) * 128, i * 128:(i + 1) * 128],
                  group=f"lh{k % 4}")
        p.mm(pmo[:, 0:96], [(hkl[k][:], wxs[:, k * 96:(k + 1) * 96])
                            for k in range(16)], name=f"sp{i}")
        p.cpy(ssp[:], pmo[:, 0:96])
        p.cpy(sspb[:], ssp[:])
        p.tr(psT[0:64, :], sspb[:, 0:64], ident[:])
        p.cpy(tsT[:, i * 128:(i + 1) * 128], psT[0:64, :])
        p.tr(psT[0:16, :], sspb[:, 64:80], ident[:])
        p.cpy(BT[:, i * 128:(i + 1) * 128], psT[0:16, :])
        p.tr(psT[0:16, :], sspb[:, 80:96], ident[:])
        p.cpy(CT[:, i * 128:(i + 1) * 128], psT[0:16, :])
    for hb in range(16):
        hsl = slice(hb * 128, (hb + 1) * 128)
        for (c0, c1) in CH:
            p.mm(pmo[:, 0:c1 - c0], [(wdt[:, hsl], tsT[:, c0:c1])],
                 name=f"dt{hb}_{c0}")
            p.ts(dtw[:, c0:c1], pmo[:, 0:c1 - c0], bdtT[:, hb:hb + 1],
                 op0=ALU.add)
        p.act(cvt[:], dtw[:], AF.Abs)
        p.act(cvt[:], cvt[:], AF.Exp, scale=-1.0)
        p.act(cvt[:], cvt[:], AF.Ln, bias=1.0)
        p.ts(dtw[:], dtw[:], 0.0, op0=ALU.max)
        p.tt(dtw[:], dtw[:], cvt[:], ALU.add)
        p.dma(d_dtT[hsl, :], dtw[:], group="sdt", eng="gpsimd")

    Bbc = [xnT[d] for d in range(8)] + Bbc8
    for d in range(DST):
        for (c0, c1) in CH:
            p.mm(pmo[:, 0:c1 - c0], [(sel[:, d * 128:(d + 1) * 128], BT[:, c0:c1])],
                 name=f"bcB{d}_{c0}")
            p.cpy(Bbc[d][:, c0:c1], pmo[:, 0:c1 - c0])

    ut = rawl
    ymt = htw
    for hb in range(16):
        hsl = slice(hb * 128, (hb + 1) * 128)
        p.dma(dtt[:], d_dtT[hsl, :], group="ldt")
        p.dma(htt[:], d_hT[hsl, :], group="lht")
        p.tt(ut[:], dtt[:], htt[:], ALU.mult)
        for d in range(DST):
            Cb = Cbc1
            for (c0, c1) in CH:
                p.mm(pmo[:, 0:c1 - c0], [(sel[:, d * 128:(d + 1) * 128], CT[:, c0:c1])],
                     name=f"bcC{hb}_{d}_{c0}")
                p.cpy(Cb[:, c0:c1], pmo[:, 0:c1 - c0])
            p.act(dA[:], dtt[:], AF.Exp, scale=asb[:, hb * 16 + d:hb * 16 + d + 1])
            p.tt(dBu[:], ut[:], Bbc[d][:], ALU.mult)
            p.scan(ssc[:], dA[:], dBu[:], 0.0, name=f"scan{hb}_{d}")
            p.tt(ymt[:], ssc[:], Cb[:], ALU.mult)
            for c, (c0, c1) in enumerate(CH):
                p.mm(psy[c][:, 0:c1 - c0], [(ident[:], ymt[:, c0:c1])],
                     name=f"ya{hb}_{d}_{c}", start=(d == 0), stop=(d == DST - 1))
        p.dma(gtt[:], d_gateT[hsl, :], group="lgt")
        p.act(gtt[:], gtt[:], AF.Silu)
        for c, (c0, c1) in enumerate(CH):
            p.stt(cvt[:, c0:c1], htt[:, c0:c1], ddT[:, hb:hb + 1],
                  psy[c][:, 0:c1 - c0], op0=ALU.mult, op1=ALU.add)
        p.tt(y2[:], cvt[:], gtt[:], ALU.mult)
        p.dma(d_y2T[hsl, :], y2[:, HALO:TL], group="sy2", eng="gpsimd")

    wob = [Bbc[k][:, 0:D] for k in range(8)] + [Bbc8[k][:, D:2 * D] for k in range(8)]
    for k in range(16):
        p.dma(winl[:, 0:D], w_out[k * 128:(k + 1) * 128, :], group="lwo")
        p.cpy(wob[k], winl[:, 0:D])
    for i in range(TCORE // 128):
        tsl = slice(i * 128, (i + 1) * 128)
        p.dma(rtile[:], resid_h[HALO + i * 128:HALO + (i + 1) * 128, :],
              group="ldr")
        for k in range(16):
            p.dma(hkl[k][:], d_y2T[k * 128:(k + 1) * 128, tsl], group=f"ly{k % 4}")
        for b in range(2):
            csl = slice(b * 512, (b + 1) * 512)
            p.mm(pmo[:], [(hkl[k][:], wob[k][:, csl]) for k in range(16)],
                 name=f"wo{i}_{b}")
            p.stt(rtile[:, csl], pmo[:], 1.0, rtile[:, csl], op0=ALU.mult,
                  op1=ALU.add)
        p.dma(resid2[tsl, :], rtile[:, 0:D], group="st")
    nc = p.finish()
    return nc, p.outs


_CACHE = {}


def _get(name):
    if name not in _CACHE:
        _CACHE[name] = dict(ffn=build_ffn, ffn2=build_ffn2, moe=build_moe, attn_a=build_attn_a,
                            attn_b=build_attn_b, mamba=build_mamba)[name]()
    return _CACHE[name]


def _consts():
    identb = np.eye(128).astype(BF)
    identf = np.eye(128, dtype=np.float32)
    sel16 = np.zeros((16, 16 * 128), BF)
    for d in range(16):
        sel16[d, d * 128:(d + 1) * 128] = 1
    e2 = np.zeros((128, 2), BF)
    e2[0:64, 0] = 1
    e2[64:128, 1] = 1
    e2t = np.ascontiguousarray(e2.T)
    return identb, identf, sel16, e2, e2t


def _launch(name, in_maps):
    nc, outs = _get(name)
    res = run_bass_kernel_spmd(nc, in_maps, core_ids=list(range(NCORES)))
    return [{k: np.asarray(r[k]) for k in outs} for r in res.results]


def kernel(L0_attn_x, L0_attn_W_in, L0_attn_conv_w, L0_attn_conv_b,
           L0_attn_W_x, L0_attn_W_dt, L0_attn_b_dt, L0_attn_A_log, L0_attn_D,
           L0_attn_W_out, L0_ffn_w_gate, L0_ffn_w_up, L0_ffn_w_down,
           L1_attn_w_qkv, L1_attn_w_proj, L1_attn_b_proj, L1_attn_w_dwc,
           L1_attn_b_dwc, L1_ffn_router_w, L1_ffn_router_b, L1_ffn_expert_w,
           L1_ffn_expert_b):
    identb, identf, sel16, e2, e2t = _consts()
    f32 = lambda a: np.ascontiguousarray(a, dtype=np.float32)
    x = f32(L0_attn_x).reshape(NCORES * TCORE, D)

    def shards(r):
        return [r[c * TCORE:(c + 1) * TCORE] for c in range(NCORES)]

    prep = {}

    def run_ffn(r):
        maps = [dict(resid=s, w_gate=f32(L0_ffn_w_gate), w_up=f32(L0_ffn_w_up),
                     w_down=f32(L0_ffn_w_down), identb=identb)
                for s in shards(r)]
        out = _launch("ffn", maps)
        prep.update(tgT=out[0]["o_tgT"], tuT=out[0]["o_tuT"],
                    tdT=out[0]["o_tdT"], wscal=out[0]["o_scal"])
        return np.concatenate([o["resid2"] for o in out])

    def run_ffn2(r):
        maps = [dict(resid=s, tgT=prep["tgT"], tuT=prep["tuT"],
                     tdT=prep["tdT"], wscal=prep["wscal"], identb=identb)
                for s in shards(r)]
        out = _launch("ffn2", maps)
        return np.concatenate([o["resid2"] for o in out])

    def run_mamba(r):
        zh = np.zeros((HALO, D), np.float32)
        maps = []
        cw = f32(L0_attn_conv_w).reshape(4, 2048)
        for c in range(NCORES):
            halo = zh if c % 4 == 0 else r[c * TCORE - HALO:c * TCORE]
            rh = np.concatenate([halo, r[c * TCORE:(c + 1) * TCORE]])
            maps.append(dict(
                resid_h=f32(rh), w_in=f32(L0_attn_W_in), conv_w=cw,
                conv_b=f32(L0_attn_conv_b).reshape(1, 2048),
                w_x=f32(L0_attn_W_x), w_dt=f32(L0_attn_W_dt),
                b_dt=f32(L0_attn_b_dt).reshape(1, 2048),
                a_log=f32(L0_attn_A_log),
                d_vec=f32(L0_attn_D).reshape(1, 2048),
                w_out=f32(L0_attn_W_out), identb=identb, identf=identf,
                sel16=sel16))
        out = _launch("mamba", maps)
        return np.concatenate([o["resid2"] for o in out])

    def run_moe(r):
        maps = [dict(resid=s, router_w=f32(L1_ffn_router_w),
                     router_b=f32(L1_ffn_router_b).reshape(1, 8),
                     expert_w=f32(L1_ffn_expert_w),
                     expert_b=f32(L1_ffn_expert_b), identb=identb)
                for s in shards(r)]
        out = _launch("moe", maps)
        return np.concatenate([o["resid2"] for o in out])

    def run_attn(r):
        mapsA = [dict(resid=s, w_qkv=f32(L1_attn_w_qkv), identb=identb,
                      e2=e2, e2t=e2t) for s in shards(r)]
        outA = _launch("attn_a", mapsA)
        qh = [o["qhat"] for o in outA]
        zc = np.zeros((D, 1), np.float32)
        mapsB = []
        for c in range(NCORES):
            g0 = (c // 4) * 4
            kvp4 = np.stack([outA[g0 + i]["kv_part"] for i in range(4)])
            ks4 = np.concatenate([outA[g0 + i]["ksum_part"] for i in range(4)])
            left = qh[c - 1][:, -1:] if c % 4 != 0 else zc
            right = qh[c + 1][:, 0:1] if c % 4 != 3 else zc
            qh_in = np.concatenate([left, qh[c], right], axis=1)
            mapsB.append(dict(resid=r[c * TCORE:(c + 1) * TCORE], qh_in=qh_in,
                              kvp4=kvp4, ks4=ks4, w_proj=f32(L1_attn_w_proj),
                              b_proj=f32(L1_attn_b_proj).reshape(1, D),
                              w_dwc=f32(L1_attn_w_dwc),
                              b_dwc=f32(L1_attn_b_dwc).reshape(1, D),
                              identb=identb))
        outB = _launch("attn_b", mapsB)
        return np.concatenate([o["resid2"] for o in outB])

    r = run_ffn(x)
    r = run_mamba(r)
    r = run_ffn2(r)
    r = run_moe(r)
    r = run_attn(r)
    r = run_moe(r)
    return r.reshape(2, 8192, D).astype(np.float32)
